# version 29
# baseline (speedup 1.0000x reference)
import numpy as np
from contextlib import ExitStack

import concourse.bass as bass
import concourse.mybir as mybir
from concourse.bass_utils import run_bass_kernel_spmd

F32 = mybir.dt.float32
BF16 = mybir.dt.bfloat16
AF = mybir.ActivationFunctionType
ALU = mybir.AluOpType
AX = mybir.AxisListType

S = 4096
DM = 1024
NH = 16
HD = 64
CH = 128
SC = 8
TS = 512
CPS = 4
KB = 8
NE = 8
NMAT = 12
EPS = 1e-5

TAPS = {}
DEBUG_SC = None
DEBUG_STAGE = None
DEBUG_SCAN = None
DEBUG_R0ZERO = False
TAP_SPECS = {"qt": ([128, 4096], BF16), "kt": ([128, 4096], BF16),
             "ktm": ([128, 4096], BF16), "vext": ([128, 4160], BF16),
             "msc": ([128, 2048], BF16), "attn": ([128, 1024], BF16),
             "sbf": ([64, 1040], BF16), "x1": ([128, 1024], F32),
             "w": ([128, 8], F32), "ffn": ([128, 1024], F32),
             "x1full": ([128, 4096], F32), "wfull": ([128, 32], F32),
             "attn1": ([128, 1024], BF16), "msc1": ([128, 2048], BF16),
             "sbf1": ([64, 1040], BF16)}


class Bld:
    ENGS = ("PE", "DVE", "ACT", "POOL", "SP")

    def __init__(self, nc, es):
        self.nc = nc
        self.streams = {k: [] for k in self.ENGS}
        self.ctr = {}
        for name in ("cpe", "cdv", "cac", "cpl", "dmc", "dwl0", "dwl1",
                     "dwl2", "dwm0", "dwm1", "dwm2", "dscr", "dxt0", "dxt1",
                     "dx00", "dx01", "dos0", "dos1", "dtap"):
            h = es.enter_context(nc.semaphore(name))
            self.ctr[name] = [h, 0]
        self.waited = {e: {} for e in self.ENGS}

    def wait(self, eng, cname, val):
        if val is None or val <= 0:
            return
        w = self.waited[eng]
        if w.get(cname, 0) >= val:
            return
        w[cname] = val
        h = self.ctr[cname][0]
        self.streams[eng].append(lambda e, h=h, v=val: e.wait_ge(h, v))

    def waits(self, eng, evs):
        for cname, val in evs:
            self.wait(eng, cname, val)

    def op(self, eng, fn, inc=None, k=1):
        if inc is None:
            self.streams[eng].append(lambda e, fn=fn: fn(e))
            return None
        c = self.ctr[inc]
        c[1] += k
        v = c[1]
        h = c[0]

        def run(e, fn=fn, h=h, k=k):
            fn(e).then_inc(h, k)

        self.streams[eng].append(run)
        return v

    def val(self, cname):
        return self.ctr[cname][1]


def build():
    nc = bass.Bass("TRN2", target_bir_lowering=False)

    dx = nc.dram_tensor("x", [S, DM], F32, kind="ExternalInput").ap()
    dxT = nc.dram_tensor("xT", [DM, S], F32, kind="ExternalInput").ap()
    dwT = [nc.dram_tensor(n, [DM, DM], F32, kind="ExternalInput").ap()
           for n in ["wqT", "wkT", "wvT", "woT"]]
    dbias = {n: nc.dram_tensor(n, [1, DM], F32, kind="ExternalInput").ap()
             for n in ["bq", "bk", "bv", "bo"]}
    dgwT = nc.dram_tensor("gwT", [DM, NE], F32, kind="ExternalInput").ap()
    dgb = nc.dram_tensor("gb", [1, NE], F32, kind="ExternalInput").ap()
    dewT = nc.dram_tensor("ewT", [NE * DM, DM], F32, kind="ExternalInput").ap()
    deb = nc.dram_tensor("eb", [1, NE * DM], F32, kind="ExternalInput").ap()
    dident = nc.dram_tensor("ident", [128, 128], F32, kind="ExternalInput").ap()
    dmaskT = nc.dram_tensor("maskT", [128, 128], F32, kind="ExternalInput").ap()
    dpmask = nc.dram_tensor("pmask", [128, 2], F32, kind="ExternalInput").ap()
    dbqc = nc.dram_tensor("bqc", [128, KB], F32, kind="ExternalInput").ap()
    dbkc = nc.dram_tensor("bkc", [128, KB], F32, kind="ExternalInput").ap()
    dout = nc.dram_tensor("out", [S, DM], F32, kind="ExternalOutput").ap()
    descr = nc.dram_tensor("escr", [NMAT * 128, 8192], BF16, kind="Internal").ap()
    dtaps = {n: nc.dram_tensor("tap_" + n, *TAP_SPECS[n], kind="ExternalOutput").ap()
             for n in TAPS if n in TAP_SPECS}

    es = ExitStack()
    sb = lambda name, shape, dt: es.enter_context(nc.sbuf_tensor(name, shape, dt))
    psal = lambda name, shape, dt: es.enter_context(nc.psum_tensor(name, shape, dt))

    wb = [sb(f"wb{j}", [128, 8192], BF16) for j in range(3)]
    b2f = wb[2][:].bitcast(F32)
    xt_land = [sb(f"xtl{j}", [128, 512], F32) for j in range(2)]
    xt_bf = sb("xt_bf", [128, 4096], BF16)
    x0c = [sb(f"x0c{j}", [128, 1024], F32) for j in range(2)]
    qt_bf = sb("qt_bf", [128, 4096], BF16)
    kt_bf = sb("kt_bf", [128, 4096], BF16)
    ktm_bf = sb("ktm_bf", [128, 4096], BF16)
    vext_bf = sb("vext_bf", [128, 4160], BF16)
    msc_bf = sb("msc_bf", [128, 2048], BF16)
    attn_bfp = [sb(f"attnb{j}", [128, 1024], BF16) for j in range(2)]
    attnT_bfp = [sb(f"attnTb{j}", [128, 1024], BF16) for j in range(2)]
    x1f = sb("x1f", [128, 4096], F32)
    x1c_bf = sb("x1c_bf", [128, 1024], BF16)
    x1t_bf = sb("x1t_bf", [128, 4096], BF16)
    o_sb = [sb(f"osb{j}", [128, 1024], F32) for j in range(2)]
    S_f = sb("S_f", [64, 1040], F32)
    S_bfp = [sb(f"Sbf{j}", [128, 1040], BF16) for j in range(2)]
    e_tmp = [sb(f"etmp{j}", [128, 512], F32) for j in range(2)]
    e_tmp2 = [sb(f"etmp2{j}", [128, 512], F32) for j in range(2)]
    id_f = sb("id_f", [128, 128], F32)
    id_bf = sb("id_bf", [128, 128], BF16)
    id_f32 = sb("id_f32", [128, 128], F32)
    maskT_f = sb("maskT_f", [128, 128], F32)
    pmask = sb("pmask_s", [128, 2], F32)
    bqc_s = sb("bqc_s", [128, KB], F32)
    bkc_s = sb("bkc_s", [128, KB], F32)
    wT_all = sb("wT_all", [8, 512], BF16)
    w_bfs = sb("w_bfs", [128, 32], BF16)
    ebias8 = sb("ebias8", [8, 1024], BF16)
    ktz = sb("ktz", [128, 2048], BF16)
    qtz = sb("qtz", [128, 2048], BF16)
    ones_r = sb("ones_r", [1, 512], F32)
    bias_s = {n: sb(n + "_s", [1, 1024], F32) for n in ["bv", "bo"]}
    gb_s = sb("gb_s", [1, NE], F32)
    gwT_f = sb("gwT_f", [128, 64], F32)
    gwT_b = sb("gwT_b", [128, 64], BF16)
    lnst = [sb(f"lnst{j}", [128, 12], F32) for j in range(2)]
    mnv = [sb(f"mnv{j}", [128, 2], F32) for j in range(2)]
    sdev = [sb(f"sdev{j}", [128, 1], F32) for j in range(2)]
    rstd = [sb(f"rstd{j}", [128, 1], F32) for j in range(2)]
    lnst2 = sb("lnst2", [128, 12], F32)
    mnv2 = sb("mnv2", [128, 2], F32)
    sdev2 = sb("sdev2", [128, 1], F32)
    rstd2 = sb("rstd2", [128, 1], F32)
    negm = [sb(f"negm{j}", [128, 1], F32) for j in range(2)]
    eg = [sb(f"eg{j}", [128, NE], F32) for j in range(2)]
    sg = [sb(f"sg{j}", [128, 1], F32) for j in range(2)]
    rg = [sb(f"rg{j}", [128, 1], F32) for j in range(2)]
    gt = [sb(f"gt{j}", [128, NE], F32) for j in range(2)]
    t8 = [sb(f"t8{j}", [128, NE], F32) for j in range(2)]
    w_all = sb("w_all", [128, 32], F32)
    rden = sb("rden", [128, 16], F32)
    eps_c = sb("eps_c", [128, 1], F32)
    magic_c = sb("magic_c", [128, 1], mybir.dt.int32)
    rsq = [sb(f"rsq{j}", [128, 6], F32) for j in range(3)]

    quad = [psal(f"q{i}", [128, 1024], F32) for i in range(4)]
    qbf = [q[:].bitcast(BF16) for q in quad]

    B = Bld(nc, es)

    wready = [[] for _ in range(NMAT)]
    wpieces = [[[] for _ in range(KB)] for _ in range(NMAT)]
    wslot = [None] * NMAT
    wbuf_free = [[], [], []]
    slot_store = [[], [], []]
    store_val = [0] * NMAT
    qfree = {}

    def qwait(qi, halves=(0, 1)):
        evs = []
        for hh in halves:
            evs += qfree.get((qi, hh), [])
        return evs

    def qset(qi, halves, evs):
        for hh in halves:
            qfree[(qi, hh)] = list(evs)

    def dma(eng, cname, out_ap, in_ap):
        return B.op(eng, lambda e, o=out_ap, i=in_ap: e.dma_start(o, i),
                    inc=cname, k=16)

    dma("SP", "dmc", id_f[:], dident)
    dma("SP", "dmc", maskT_f[:], dmaskT)
    dma("SP", "dmc", pmask[:], dpmask)
    dma("SP", "dmc", bqc_s[:], dbqc)
    dma("SP", "dmc", bkc_s[:], dbkc)
    for n in ["bv", "bo"]:
        dma("SP", "dmc", bias_s[n][:], dbias[n])
    dma("SP", "dmc", gb_s[:], dgb)
    dma("SP", "dmc", gwT_f[:].rearrange("p (k e) -> p k e", e=NE),
        dgwT.rearrange("(k p) e -> p k e", p=128))
    v_cdone = B.val("dmc")
    v_eb0 = dma("SP", "dwl2", b2f[0:8, 0:1024],
                deb.rearrange("o (e h) -> (o e) h", h=DM))

    B.op("POOL", lambda e: e.memset(eps_c[:], EPS), inc="cpl")
    B.op("POOL", lambda e: e.memset(magic_c[:], 0x5F3759DF), inc="cpl")
    B.op("POOL", lambda e: e.memset(ones_r[:], 1.0), inc="cpl")
    B.op("POOL", lambda e: e.memset(vext_bf[:], 1.0), inc="cpl")
    B.op("POOL", lambda e: e.memset(S_f[:], 0.0), inc="cpl")
    B.op("POOL", lambda e: e.memset(S_bfp[0][:], 0.0), inc="cpl")
    v_ms = B.op("POOL", lambda e: e.memset(S_bfp[1][:], 0.0), inc="cpl")

    B.wait("DVE", "dmc", v_cdone)
    v_idb = B.op("DVE", lambda e: e.tensor_copy(id_bf[:], id_f[:]), inc="cdv")
    v_idf = B.op("DVE", lambda e: e.tensor_copy(id_f32[:], id_f[:]), inc="cdv")
    v_gwb = B.op("DVE", lambda e: e.tensor_copy(gwT_b[:], gwT_f[:]), inc="cdv")
    B.wait("DVE", "dwl2", v_eb0)
    v_ebc1 = B.op("DVE", lambda e: e.tensor_copy(ebias8[:], b2f[0:8, 0:1024]),
                  inc="cdv")

    WSCHED = [(s, m) for s in range(SC) for m in range(NMAT)]
    wptr = [0]
    land_free = [[("cdv", v_ebc1)]]

    def w_src_half(m, q):
        if m < 4:
            return dwT[m][q * 512:(q + 1) * 512, :].rearrange(
                "(k p) o -> p k o", p=128)
        e = m - 4
        return dewT[e * DM + q * 512: e * DM + (q + 1) * 512, :].rearrange(
            "(k p) o -> p k o", p=128)

    def feed_weight(slot):
        if wptr[0] >= len(WSCHED):
            return
        s, m = WSCHED[wptr[0]]
        wptr[0] += 1
        dst = wb[slot]
        if s == 0:
            ws = []
            for q in range(2):
                B.waits("SP", land_free[0])
                vldw = dma("SP", "dwl2",
                           b2f[:].rearrange("p (k o) -> p k o", o=1024),
                           w_src_half(m, q))
                eng = "DVE" if (m + q) % 2 == 0 else "ACT"
                cn = "cdv" if eng == "DVE" else "cac"
                B.wait(eng, "dwl2", vldw)
                B.waits(eng, wbuf_free[slot])
                if eng == "DVE":
                    vc = B.op(eng, lambda e, d=dst, q=q: e.tensor_copy(
                        d[:, q * 4096:(q + 1) * 4096], b2f[:]), inc=cn)
                else:
                    vc = B.op(eng, lambda e, d=dst, q=q: e.activation(
                        d[:, q * 4096:(q + 1) * 4096], b2f[:], AF.Copy), inc=cn)
                ws.append((cn, vc))
                land_free[0] = [(cn, vc)]
            wready[m] = list(ws)
            for k in range(KB):
                wpieces[m][k] = [ws[k // 4]]
            wslot[m] = slot
            B.waits("SP", ws)
            B.wait("SP", "dscr", B.val("dscr"))
            store_val[m] = dma("SP", "dscr", descr[m * 128:(m + 1) * 128, :], dst[:])
            slot_store[slot] = [("dscr", store_val[m])]
        else:
            B.waits("SP", wbuf_free[slot])
            B.wait("SP", "dscr", store_val[m])
            B.waits("POOL", wbuf_free[slot])
            B.wait("POOL", "dscr", store_val[m])
            if m < 2:
                evs = []
                for k in range(KB):
                    eng = "SP" if k % 2 == 0 else "POOL"
                    cn = (f"dwl{slot}" if k % 2 == 0 else f"dwm{slot}")
                    B.wait(eng, cn, B.val(cn))
                    v = dma(eng, cn, wb[slot][:, k * 1024:(k + 1) * 1024],
                            descr[m * 128:(m + 1) * 128,
                                  k * 1024:(k + 1) * 1024])
                    wpieces[m][k] = [(cn, v)]
                    evs.append((cn, v))
                wready[m] = [evs[-1], evs[-2]]
            else:
                v1 = dma("SP", f"dwl{slot}", wb[slot][:, 0:4096],
                         descr[m * 128:(m + 1) * 128, 0:4096])
                v2 = dma("POOL", f"dwm{slot}", wb[slot][:, 4096:8192],
                         descr[m * 128:(m + 1) * 128, 4096:8192])
                wready[m] = [(f"dwl{slot}", v1), (f"dwm{slot}", v2)]
                for k in range(KB):
                    wpieces[m][k] = wready[m]
            wslot[m] = slot

    def free_and_feed(mat):
        slot = wslot[mat]
        done = B.val("cpe")
        wbuf_free[slot] = [("cpe", done)] + slot_store[slot]
        slot_store[slot] = []
        feed_weight(slot)

    xt_cast = [0] * KB
    x1c_free = [[]]
    xtl_free = [[], []]
    xtbf_free = [[]]
    x0_dma = [0, 0]
    x0_free = [[], []]
    osb_free = [[], []]
    qt_done = [0] * KB
    kt_done = [0] * KB
    ktm_done = [("cac", 0)] * CPS
    vext_done = [0] * CPS
    etmp_free = [[], []]
    etmp2_free = [[], []]
    sbf_cast = {-1: ("cpl", v_ms)}
    sbf_cast2 = {-1: ("cpl", v_ms)}
    msc_done = [0, 0]
    ktz_free = [[]]
    qtz_free = [[]]
    kqz_done = [(None, 0, 0)]
    div_done = [0, 0]
    x1t_evac = [0] * CPS
    x1_apply = [0] * CPS
    w_done = [0] * CPS
    w_bf_done = [0] * CPS
    wT_done = [0] * CPS
    stt_val = [None] * CPS
    stt_guard = [0] * CPS
    vxb_prev = [0]
    tail_state = {}

    tap_guard = [0]
    stage_holder = [99]

    def emit_tap(name, src_ap, evs):
        if name not in dtaps:
            return
        B.waits("SP", evs)
        B.wait("SP", "dtap", B.val("dtap"))
        tap_guard[0] = dma("SP", "dtap", dtaps[name], src_ap)

    def x_stream(s):
        for engx in ("PE", "DVE", "ACT", "POOL"):
            B.wait(engx, "dtap", tap_guard[0])
        B.wait("PE", "cpl", v_ms)
        B.wait("PE", "dmc", v_cdone)
        B.wait("ACT", "cpl", v_ms)
        B.wait("DVE", "cpl", v_ms)
        for k in range(KB):
            j = k % 2
            B.waits("POOL", xtl_free[j])
            vd = dma("POOL", f"dxt{j}", xt_land[j][:],
                     dxT[k * 128:(k + 1) * 128, s * 512:(s + 1) * 512])
            B.wait("DVE", f"dxt{j}", vd)
            B.waits("DVE", xtbf_free[0])
            vc = B.op("DVE", lambda e, k=k, j=j: e.tensor_copy(
                xt_bf[:, k * 512:(k + 1) * 512], xt_land[j][:]), inc="cdv")
            xt_cast[k] = vc
            xtl_free[j] = [("cdv", vc)]

    def proj_core(s):
        def proj_qk(mat, dstbf, donearr, bias_col):
            src = wb[wslot[mat]]
            vpe_f = [None] * KB
            for wv in range(2):
                fs = range(wv * 4, wv * 4 + 4)
                for k in range(KB):
                    B.waits("PE", wpieces[mat][k])
                    B.wait("PE", "cdv", xt_cast[k])
                    if k == 0:
                        for f in fs:
                            B.waits("PE", qwait(2 + (f % 4) // 2, ((f % 4) % 2,)))
                    for f in fs:
                        fn = lambda e, src=src, k=k, f=f: e.matmul(
                            quad[2 + (f % 4) // 2][
                                :, ((f % 4) % 2) * 512:((f % 4) % 2 + 1) * 512],
                            src[:, k * 1024 + f * 128: k * 1024 + (f + 1) * 128],
                            xt_bf[:, k * 512:(k + 1) * 512],
                            start=(k == 0), stop=(k == KB - 1))
                        if k == KB - 1:
                            vpe_f[f] = B.op("PE", fn, inc="cpe")
                        else:
                            B.op("PE", fn)
                for f in fs:
                    emit_phi(mat, dstbf, donearr, bias_col, f, vpe_f[f])
                ffn_drain(s_cur[0], limit=2)
            free_and_feed(mat)

        def emit_phi(mat, dstbf, donearr, bias_col, f, vpe):
            if True:
                qi, hh = 2 + (f % 4) // 2, (f % 4) % 2
                et = e_tmp[f % 2]
                et2 = e_tmp2[f % 2]
                bc = bias_col[:, f:f + 1]
                B.wait("ACT", "cpe", vpe)
                B.waits("ACT", etmp_free[f % 2])
                va = B.op("ACT", lambda e, qi=qi, hh=hh, et=et, bc=bc: e.activation(
                    et[:], quad[qi][:, hh * 512:(hh + 1) * 512], AF.Exp,
                    bias=bc), inc="cac")
                B.waits("ACT", etmp2_free[f % 2])
                va2 = B.op("ACT", lambda e, qi=qi, hh=hh, et2=et2, bc=bc:
                           e.activation(
                    et2[:], quad[qi][:, hh * 512:(hh + 1) * 512], AF.Relu,
                    bias=bc), inc="cac")
                qset(qi, (hh,), [("cac", va2)])
                B.wait("DVE", "cac", va2)
                vd = B.op("DVE", lambda e, et=et, et2=et2, f=f, d=dstbf:
                          e.scalar_tensor_tensor(
                              d[:, f * 512:(f + 1) * 512],
                              et[:], 1.0, et2[:], ALU.min, ALU.add), inc="cdv")
                donearr[f] = vd
                etmp_free[f % 2] = [("cdv", vd)]
                etmp2_free[f % 2] = [("cdv", vd)]

        proj_qk(0, qt_bf, qt_done, bqc_s)
        proj_qk(1, kt_bf, kt_done, bkc_s)
        if (DEBUG_SCAN or 99) >= 1 and stage_holder[0] >= 2:
            build_zmask(0)
            build_qz(0)

        for c in range(CPS):
            qi = c % 2
            B.waits("PE", qwait(qi))
            B.waits("PE", wready[2])
            src = wb[wslot[2]]
            vpe = None
            for k in range(KB):
                for hh in range(2):
                    B.op("PE", lambda e, src=src, k=k, c=c, qi=qi, hh=hh: e.matmul(
                        quad[qi][:, hh * 512:(hh + 1) * 512],
                        xt_bf[:, k * 512 + c * 128: k * 512 + (c + 1) * 128],
                        src[:, k * 1024 + hh * 512: k * 1024 + (hh + 1) * 512],
                        start=(k == 0), stop=False))
            for hh in range(2):
                fn = lambda e, qi=qi, hh=hh: e.matmul(
                    quad[qi][:, hh * 512:(hh + 1) * 512],
                    ones_r[0:1, 0:128],
                    bias_s["bv"][0:1, hh * 512:(hh + 1) * 512],
                    start=False, stop=True)
                if hh == 0:
                    B.op("PE", fn)
                else:
                    vpe = B.op("PE", fn, inc="cpe")
            B.wait("ACT", "cpe", vpe)
            va = B.op("ACT", lambda e, c=c, qi=qi: e.activation(
                vext_bf[:, c * 1040:(c + 1) * 1040].rearrange(
                    "p (h w) -> p h w", w=65)[:, :, 0:64],
                quad[qi][:].rearrange("p (h w) -> p h w", w=64),
                AF.Copy), inc="cac")
            vext_done[c] = va
            qset(qi, (0, 1), [("cac", va)])
        free_and_feed(2)

        B.wait("PE", "cdv", v_idb)
        for c in range(CPS):
            qi = (c + 2) % 4
            B.waits("PE", qwait(qi))
            vpe = None
            for f in range(KB):
                B.wait("PE", "cdv", kt_done[f])
                fn = lambda e, c=c, f=f, qi=qi: e.matmul(
                    quad[qi][:, f * 128:(f + 1) * 128],
                    kt_bf[:, f * 512 + c * 128: f * 512 + (c + 1) * 128],
                    id_bf[:], start=True, stop=True)
                if f < KB - 1:
                    B.op("PE", fn)
                else:
                    vpe = B.op("PE", fn, inc="cpe")
            B.wait("DVE", "cpe", vpe)
            vd = B.op("DVE", lambda e, c=c, qi=qi: e.tensor_copy(
                ktm_bf[:, c * 1024:(c + 1) * 1024], quad[qi][:]), inc="cdv")
            ktm_done[c] = ("cdv", vd)
            qset(qi, (0, 1), [("cdv", vd)])

    def rsqrt_dve(var_ap, out_ap, scr, wait_ev):
        B.waits("DVE", wait_ev)
        ve = scr[:, 0:1]
        vh = scr[:, 1:2]
        ya = scr[:, 2:3]
        yb = scr[:, 3:4]
        yc = scr[:, 4:5]
        yy = scr[:, 5:6]
        v = B.op("DVE", lambda e: e.tensor_scalar(
            ve, var_ap, eps_c[:, 0:1], None, ALU.add), inc="cdv")
        B.wait("DVE", "cdv", v)
        v = B.op("DVE", lambda e: e.tensor_scalar(
            vh, ve, 0.5, None, ALU.mult), inc="cdv")
        B.wait("DVE", "cdv", v)
        v = B.op("DVE", lambda e: e.tensor_scalar(
            yb.bitcast(mybir.dt.int32), ve.bitcast(mybir.dt.int32), 1, None,
            ALU.arith_shift_right), inc="cdv")
        B.wait("DVE", "cdv", v)
        v = B.op("DVE", lambda e: e.tensor_tensor(
            yy.bitcast(mybir.dt.int32), magic_c[:, 0:1],
            yb.bitcast(mybir.dt.int32), ALU.subtract), inc="cdv")
        for _ in range(2):
            B.wait("DVE", "cdv", v)
            v = B.op("DVE", lambda e: e.tensor_tensor(ya, yy, yy, ALU.mult),
                     inc="cdv")
            B.wait("DVE", "cdv", v)
            v = B.op("DVE", lambda e: e.tensor_tensor(yb, ya, vh, ALU.mult),
                     inc="cdv")
            B.wait("DVE", "cdv", v)
            v = B.op("DVE", lambda e: e.tensor_scalar(
                yc, yb, 1.5, -1.0, ALU.subtract, op1=ALU.mult), inc="cdv")
            B.wait("DVE", "cdv", v)
            v = B.op("DVE", lambda e: e.tensor_tensor(yy, yy, yc, ALU.mult),
                     inc="cdv")
        B.wait("DVE", "cdv", v)
        v = B.op("DVE", lambda e: e.tensor_copy(out_ap, yy), inc="cdv")
        return v

    def build_zmask(c):
        B.wait("ACT", "cdv", max(kt_done))
        B.waits("ACT", ktz_free[0])
        B.wait("ACT", "dmc", v_cdone)
        vkz = None
        for par in range(2):
            mcol = pmask[:, par:par + 1]
            vkz = B.op("ACT", lambda e, c=c, par=par, mcol=mcol: e.activation(
                ktz[:].rearrange("p (h w) -> p h w", w=128)[:, par::2, :],
                kt_bf[:].rearrange("p (f w) -> p f w", w=512)[:, :, c * 128:(c + 1) * 128],
                AF.Copy, scale=mcol), inc="cac")
        kqz_done[0] = ("ACT", vkz, kqz_done[0][2])

    def build_qz(c):
        B.wait("DVE", "cdv", max(qt_done))
        B.waits("DVE", qtz_free[0])
        B.wait("DVE", "dmc", v_cdone)
        vqz = None
        for par in range(2):
            mcol = pmask[:, par:par + 1]
            vqz = B.op("DVE", lambda e, c=c, par=par, mcol=mcol: e.tensor_scalar(
                qtz[:].rearrange("p (h w) -> p h w", w=128)[:, par::2, :],
                qt_bf[:].rearrange("p (f w) -> p f w", w=512)[:, :, c * 128:(c + 1) * 128],
                mcol, None, ALU.mult), inc="cdv")
        kqz_done[0] = (kqz_done[0][0], kqz_done[0][1], vqz)

    def scores_part(s, c):
        g = 4 * s + c
        dbg = DEBUG_SCAN or 99
        if dbg == 6:
            return
        for half in range(2):
            qi = half if dbg == 8 else 2 + half
            B.waits("PE", qwait(qi))
            B.wait("DVE", "dtap", tap_guard[0])
            B.wait("PE", "cac", kqz_done[0][1])
            B.wait("PE", "cdv", kqz_done[0][2])
            vsc = None
            for hh in range(8):
                h = half * 8 + hh
                f = h // 2
                B.wait("PE", "cdv", qt_done[f])
                fn = lambda e, h=h, f=f, c=c, qi=qi, hh=hh: e.matmul(
                    quad[qi][:, hh * 128:(hh + 1) * 128],
                    ktz[:, h * 128:(h + 1) * 128],
                    qt_bf[:, f * 512 + c * 128: f * 512 + (c + 1) * 128],
                    start=True, stop=True)
                if hh < 7:
                    B.op("PE", fn)
                else:
                    vsc = B.op("PE", fn, inc="cpe")
            B.wait("DVE", "cpe", vsc)
            B.wait("DVE", "dmc", v_cdone)
            if dbg == 4:
                vm = B.op("DVE", lambda e, half=half, qi=qi: e.tensor_copy(
                    msc_bf[:, half * 1024:(half + 1) * 1024], quad[qi][:]),
                    inc="cdv")
            elif dbg in (7, 8, 9):
                vm = B.op("DVE", lambda e, half=half: e.tensor_copy(
                    msc_bf[:, half * 1024:(half + 1) * 1024],
                    kt_bf[:, 0:1024]), inc="cdv")
            else:
                vm = B.op("DVE", lambda e, half=half, qi=qi: e.tensor_tensor(
                    msc_bf[:, half * 1024:(half + 1) * 1024].rearrange(
                        "p (h w) -> p h w", w=128),
                    quad[qi][:].rearrange("p (h w) -> p h w", w=128),
                    maskT_f[:].unsqueeze(1).broadcast_to([128, 8, 128]),
                    ALU.mult), inc="cdv")
            msc_done[half] = vm
        ktz_free[0] = [("cpe", B.val("cpe"))]
        if c < CPS - 1:
            build_zmask(c + 1)
        if s == 0 and c == 0:
            emit_tap("msc", msc_bf[:], [("cdv", msc_done[0]), ("cdv", msc_done[1])])
        if s == 0 and c == 1:
            emit_tap("msc1", msc_bf[:], [("cdv", msc_done[0]), ("cdv", msc_done[1])])

    def sz_part(s, c):
        g = 4 * s + c
        dbg = DEBUG_SCAN or 99
        j = c % 2
        if dbg not in (4, 5, 7, 8, 9):
            B.waits("POOL", x0_free[j])
            x0_dma[j] = dma("POOL", f"dx0{j}", x0c[j][:],
                            dx[g * 128:(g + 1) * 128, :])
        if dbg < 2 or dbg in (4, 5, 6, 7, 8, 9):
            sbf_cast[g] = sbf_cast[g - 1]
            return
        B.waits("PE", qwait(1))
        B.waits("PE", qwait(0, (0,)))
        B.wait("PE", "cac", vext_done[c])
        B.wait("PE", ktm_done[c][0], ktm_done[c][1])
        for h in range(NH):
            B.op("PE", lambda e, h=h, c=c: e.matmul(
                quad[1][0:64, h * 64:(h + 1) * 64],
                ktm_bf[:, c * 1024 + h * 64: c * 1024 + (h + 1) * 64],
                vext_bf[:, c * 1040 + h * 65: c * 1040 + h * 65 + 64],
                start=True, stop=True))
        vsz = None
        for h in range(NH):
            fn = lambda e, h=h, c=c: e.matmul(
                quad[0][0:64, h:h + 1],
                ktm_bf[:, c * 1024 + h * 64: c * 1024 + (h + 1) * 64],
                vext_bf[:, c * 1040 + h * 65 + 64: c * 1040 + h * 65 + 65],
                start=True, stop=True)
            if h < NH - 1:
                B.op("PE", fn)
            else:
                vsz = B.op("PE", fn, inc="cpe")
        B.wait("DVE", "cpe", vsz)
        cnp, vvp = sbf_cast[g - 1]
        B.wait("DVE", cnp, vvp)
        cnp2, vvp2 = sbf_cast2[g - 1]
        B.wait("DVE", cnp2, vvp2)
        B.op("DVE", lambda e: e.tensor_tensor(
            S_f[:].rearrange("p (h w) -> p h w", w=65)[:, :, 0:64],
            quad[1][0:64, :].rearrange("p (h w) -> p h w", w=64),
            S_f[:].rearrange("p (h w) -> p h w", w=65)[:, :, 0:64],
            ALU.add), inc="cdv")
        vsadd = B.op("DVE", lambda e: e.tensor_tensor(
            S_f[:].rearrange("p (h w) -> p h w", w=65)[:, :, 64:65],
            quad[0][0:64, 0:16].unsqueeze(2),
            S_f[:].rearrange("p (h w) -> p h w", w=65)[:, :, 64:65],
            ALU.add), inc="cdv")
        qset(1, (0, 1), [("cdv", vsadd)])
        qset(0, (0,), [("cdv", vsadd)])
        B.wait("ACT", "cdv", vsadd)
        B.wait("ACT", "dtap", tap_guard[0])
        B.wait("DVE", "dtap", tap_guard[0])
        vsc1 = B.op("ACT", lambda e, g=g: e.activation(
            S_bfp[g % 2][0:64, :], S_f[:], AF.Copy), inc="cac")
        B.wait("DVE", "cdv", vsadd)
        vscast = B.op("DVE", lambda e, g=g: e.tensor_copy(
            S_bfp[g % 2][64:128, :], S_f[:]), inc="cdv")
        sbf_cast[g] = ("cdv", vscast)
        sbf_cast2[g] = ("cac", vsc1)
        if s == 0 and c == 0:
            emit_tap("sbf", S_bfp[0][0:64, :], [("cac", vsc1), ("cdv", vscast)])
        if s == 0 and c == 1:
            emit_tap("sbf1", S_bfp[1][0:64, :], [("cac", vsc1), ("cdv", vscast)])

    def numden_div(s, c):
        if (DEBUG_SCAN or 99) < 3 or (DEBUG_SCAN or 99) in (4, 5, 6, 7, 8, 9):
            return []
        g = 4 * s + c
        out_evs = []
        for half in range(2):
            qi = 2 + half
            B.wait("PE", "cdv", msc_done[half])
            cn, vv = sbf_cast[g - 1]
            B.wait("PE", cn, vv)
            cn2, vv2 = sbf_cast2[g - 1]
            B.wait("PE", cn2, vv2)
            vnd = None
            B.wait("PE", "cdv", kqz_done[0][2])
            B.wait("PE", "cac", kqz_done[0][1])
            for hh in range(8):
                h = half * 8 + hh
                B.op("PE", lambda e, h=h, hh=hh, c=c, qi=qi: e.matmul(
                    quad[qi][:, hh * 128: hh * 128 + 65],
                    msc_bf[:, h * 128:(h + 1) * 128],
                    vext_bf[:, c * 1040 + h * 65: c * 1040 + (h + 1) * 65],
                    start=True, stop=False))
                fn = lambda e, h=h, hh=hh, c=c, qi=qi, g=g: e.matmul(
                    quad[qi][:, hh * 128: hh * 128 + 65],
                    qtz[:, h * 128:(h + 1) * 128],
                    S_bfp[(g - 1) % 2][:, h * 65:(h + 1) * 65],
                    start=False, stop=True)
                if hh < 7:
                    B.op("PE", fn)
                else:
                    vnd = B.op("PE", fn, inc="cpe")
            B.wait("DVE", "cpe", vnd)
            B.wait("DVE", "dtap", tap_guard[0])
            vr = B.op("DVE", lambda e, half=half, qi=qi: e.reciprocal(
                rden[:, half * 8:(half + 1) * 8].unsqueeze(2),
                quad[qi][:].rearrange("p (h w) -> p h w", w=128)[:, :, 64:65]),
                inc="cdv")
            B.wait("DVE", "cdv", vr)
            vd = B.op("DVE", lambda e, half=half, qi=qi, c=c: e.tensor_tensor(
                attn_bfp[c % 2][:, half * 512:(half + 1) * 512].rearrange(
                    "p (h w) -> p h w", w=64),
                quad[qi][:].rearrange("p (h w) -> p h w", w=128)[:, :, 0:64],
                rden[:, half * 8:(half + 1) * 8].unsqueeze(2).broadcast_to(
                    [128, 8, 64]),
                ALU.mult), inc="cdv")
            div_done[half] = vd
            qset(qi, (0, 1), [("cdv", vd)])
            out_evs.append(("cdv", vd))
        qtz_free[0] = [("cpe", B.val("cpe"))]
        if c < CPS - 1:
            build_qz(c + 1)
        if s == 0 and c == 0:
            emit_tap("attn", attn_bfp[0][:], out_evs)
        if s == 0 and c == 1:
            emit_tap("attn1", attn_bfp[1][:], out_evs)
        return out_evs

    def tail_chunk(s, c, attn_evs):
        g = 4 * s + c
        j = c % 2
        B.waits("PE", qwait(0))
        B.waits("PE", attn_evs)
        B.wait("PE", "cdv", v_idb)
        vat = None
        for f in range(KB):
            fn = lambda e, f=f, j=j: e.matmul(
                quad[0][:, f * 128:(f + 1) * 128],
                attn_bfp[j][:, f * 128:(f + 1) * 128], id_bf[:],
                start=True, stop=True)
            if f < KB - 1:
                B.op("PE", fn)
            else:
                vat = B.op("PE", fn, inc="cpe")
        B.wait("ACT", "cpe", vat)
        vae = B.op("ACT", lambda e, j=j: e.activation(
            attnT_bfp[j][:], quad[0][:], AF.Copy), inc="cac")
        B.wait("PE", "cac", vae)
        B.waits("PE", qwait(0, (0,)))
        B.waits("PE", wready[3])
        src = wb[wslot[3]]
        vP = None
        for k in range(KB):
            for hh in range(2):
                B.op("PE", lambda e, src=src, k=k, hh=hh, j=j: e.matmul(
                    quad[0][:, hh * 512:(hh + 1) * 512],
                    attnT_bfp[j][:, k * 128:(k + 1) * 128],
                    src[:, k * 1024 + hh * 512: k * 1024 + (hh + 1) * 512],
                    start=(k == 0), stop=False))
        for hh in range(2):
            fn = lambda e, hh=hh: e.matmul(
                quad[0][:, hh * 512:(hh + 1) * 512],
                ones_r[0:1, 0:128],
                bias_s["bo"][0:1, hh * 512:(hh + 1) * 512],
                start=False, stop=True)
            if hh == 0:
                B.op("PE", fn)
            else:
                vP = B.op("PE", fn, inc="cpe")
        if c == CPS - 1:
            free_and_feed(3)
        B.wait("DVE", "cpe", vP)
        B.wait("DVE", f"dx0{j}", x0_dma[j])
        B.wait("DVE", "cdv", stt_guard[c])
        vres = B.op("DVE", lambda e, j=j, c=c: e.tensor_tensor(
            x1f[:, c * 1024:(c + 1) * 1024], quad[0][:], x0c[j][:], ALU.add),
            inc="cdv")
        x0_free[j] = [("cdv", vres)]
        qset(0, (0, 1), [("cdv", vres)])
        B.wait("DVE", "cdv", vres)
        B.op("DVE", lambda e, j=j, c=c: e.bn_stats(
            lnst[j][:, 0:6], x1f[:, c * 1024: c * 1024 + 512]), inc="cdv")
        vb = B.op("DVE", lambda e, j=j, c=c: e.bn_stats(
            lnst[j][:, 6:12], x1f[:, c * 1024 + 512:(c + 1) * 1024]), inc="cdv")
        B.wait("DVE", "cdv", vb)
        vagg = B.op("DVE", lambda e, j=j: e.bn_aggr(mnv[j][:], lnst[j][:]),
                    inc="cdv")
        vrs = rsqrt_dve(mnv[j][:, 1:2], rstd[j][:], rsq[j],
                        [("cdv", vagg)])
        B.wait("DVE", "cdv", vrs)
        vx1 = B.op("DVE", lambda e, c=c, j=j: e.tensor_scalar(
            x1f[:, c * 1024:(c + 1) * 1024], x1f[:, c * 1024:(c + 1) * 1024],
            mnv[j][:, 0:1], rstd[j][:, 0:1], ALU.subtract, op1=ALU.mult),
            inc="cdv")
        x1_apply[c] = vx1
        if s == 0 and c == 0:
            emit_tap("x1", x1f[:, 0:1024], [("cdv", vx1)])
        tail_state[c] = vx1

    def tail_b(s, c):
        vx1 = tail_state[c]
        j = c % 2
        B.wait("ACT", "cdv", vx1)
        B.waits("ACT", x1c_free[0])
        vcast = B.op("ACT", lambda e, c=c: e.activation(
            x1c_bf[:], x1f[:, c * 1024:(c + 1) * 1024], AF.Copy), inc="cac")
        B.wait("PE", "cac", vcast)
        B.waits("PE", qwait(0))
        B.wait("PE", "cdv", v_idb)
        vxt = None
        for f in range(KB):
            fn = lambda e, f=f: e.matmul(
                quad[0][:, f * 128:(f + 1) * 128],
                x1c_bf[:, f * 128:(f + 1) * 128], id_bf[:],
                start=True, stop=True)
            if f < KB - 1:
                B.op("PE", fn)
            else:
                vxt = B.op("PE", fn, inc="cpe")
        x1c_free[0] = [("cpe", vxt)]
        B.wait("ACT", "cpe", vxt)
        vxe = B.op("ACT", lambda e, c=c: e.activation(
            x1t_bf[:, c * 1024:(c + 1) * 1024], quad[0][:], AF.Copy),
            inc="cac")
        x1t_evac[c] = vxe
        B.wait("PE", "cac", vxe)
        B.wait("PE", "cdv", v_gwb)
        for k in range(KB):
            B.op("PE", lambda e, k=k, c=c: e.matmul(
                quad[0][:, 0:8],
                x1t_bf[:, c * 1024 + k * 128: c * 1024 + (k + 1) * 128],
                gwT_b[:, k * 8:(k + 1) * 8],
                start=(k == 0), stop=False))
        vg = B.op("PE", lambda e: e.matmul(
            quad[0][:, 0:8], ones_r[0:1, 0:128], gb_s[0:1, :],
            start=False, stop=True), inc="cpe")
        B.wait("DVE", "cpe", vg)
        vneg = B.op("DVE", lambda e, j=j: e.reduce_max(
            negm[j][:], quad[0][:, 0:8], AX.X, negate=True), inc="cdv")
        B.wait("ACT", "cdv", vneg)
        B.wait("ACT", "cpe", vg)
        veg = B.op("ACT", lambda e, j=j: e.activation(
            eg[j][:], quad[0][:, 0:8], AF.Exp, bias=negm[j][:, 0:1],
            accum_out=sg[j][:]), inc="cac")
        B.wait("DVE", "cac", veg)
        vrg = B.op("DVE", lambda e, j=j: e.reciprocal(rg[j][:], sg[j][:]),
                   inc="cdv")
        B.wait("DVE", "cdv", vrg)
        vgt = B.op("DVE", lambda e, j=j: e.tensor_scalar(
            gt[j][:], eg[j][:], rg[j][:, 0:1], None, ALU.mult), inc="cdv")
        B.wait("DVE", "cdv", vgt)
        vt8 = B.op("DVE", lambda e, j=j: e.max(t8[j][:], gt[j][:]), inc="cdv")
        B.wait("DVE", "cdv", vt8)
        vw = B.op("DVE", lambda e, c=c, j=j: e.scalar_tensor_tensor(
            w_all[:, c * 8:(c + 1) * 8], gt[j][:], t8[j][:, 1:2], gt[j][:],
            ALU.is_ge, ALU.mult), inc="cdv")
        w_done[c] = vw
        if s == 0 and c == 0:
            emit_tap("w", w_all[:, 0:8], [("cdv", vw)])
        B.wait("DVE", "cdv", vw)
        vwb = B.op("DVE", lambda e, c=c: e.tensor_copy(
            w_bfs[:, c * 8:(c + 1) * 8], w_all[:, c * 8:(c + 1) * 8]),
            inc="cdv")
        w_bf_done[c] = vwb
        B.wait("PE", "cdv", vwb)
        B.wait("PE", "cac", veg)
        B.wait("PE", "cdv", vneg)
        vtr = B.op("PE", lambda e, c=c: e.matmul(
            quad[0][0:8, 0:128], w_bfs[:, c * 8:(c + 1) * 8], id_bf[:],
            start=True, stop=True), inc="cpe")
        B.wait("DVE", "cpe", vtr)
        vev = B.op("DVE", lambda e, c=c: e.tensor_copy(
            wT_all[:, c * 128:(c + 1) * 128], quad[0][0:8, 0:128]), inc="cdv")
        wT_done[c] = vev
        qset(0, (0,), [("cdv", vev)])

    ffnq = []
    ffn_done_c = [-1]

    def acc_wait_dve(c):
        if stt_val[c] is None:
            B.wait("DVE", "cdv", x1_apply[c])
        else:
            B.wait("DVE", "cdv", stt_val[c])

    def emit_ffn_bias(c, qi):
        B.waits("PE", qwait(qi))
        B.wait("PE", "cdv", wT_done[c])
        B.wait("PE", "cdv", v_ebc1)
        vpe = None
        for hh in range(2):
            fn = lambda e, c=c, qi=qi, hh=hh: e.matmul(
                quad[qi][:, hh * 512:(hh + 1) * 512],
                wT_all[:, c * 128:(c + 1) * 128],
                ebias8[:, hh * 512:(hh + 1) * 512],
                start=True, stop=True)
            if hh == 0:
                B.op("PE", fn)
            else:
                vpe = B.op("PE", fn, inc="cpe")
        B.wait("DVE", "cpe", vpe)
        acc_wait_dve(c)
        B.wait("DVE", "dtap", tap_guard[0])
        vstt = B.op("DVE", lambda e, c=c, qi=qi: e.scalar_tensor_tensor(
            x1f[:, c * 1024:(c + 1) * 1024], quad[qi][:], 1.0,
            x1f[:, c * 1024:(c + 1) * 1024], ALU.mult, ALU.add), inc="cdv")
        stt_val[c] = vstt
        qset(qi, (0, 1), [("cdv", vstt)])

    def emit_ffn_exp(s, e_i, c, qi):
        mat = 4 + e_i
        src = wb[wslot[mat]]
        B.waits("PE", qwait(qi))
        B.waits("PE", wready[mat])
        B.wait("PE", "cac", x1t_evac[c])
        vpe = None
        for k in range(KB):
            for hh in range(2):
                fn = lambda e, src=src, k=k, c=c, qi=qi, hh=hh: e.matmul(
                    quad[qi][:, hh * 512:(hh + 1) * 512],
                    x1t_bf[:, c * 1024 + k * 128: c * 1024 + (k + 1) * 128],
                    src[:, k * 1024 + hh * 512: k * 1024 + (hh + 1) * 512],
                    start=(k == 0), stop=(k == KB - 1))
                if hh == 1 and k == KB - 1:
                    vpe = B.op("PE", fn, inc="cpe")
                else:
                    B.op("PE", fn)
        B.wait("DVE", "cpe", vpe)
        B.wait("DVE", "cdv", w_done[c])
        acc_wait_dve(c)
        B.wait("DVE", "dtap", tap_guard[0])
        vstt = B.op("DVE", lambda e, c=c, qi=qi, e_i=e_i:
                    e.scalar_tensor_tensor(
                        x1f[:, c * 1024:(c + 1) * 1024],
                        quad[qi][:],
                        w_all[:, c * 8 + e_i: c * 8 + e_i + 1],
                        x1f[:, c * 1024:(c + 1) * 1024],
                        ALU.mult, ALU.add), inc="cdv")
        stt_val[c] = vstt
        qset(qi, (0, 1), [("cdv", vstt)])
        if s == 0 and c == 0 and e_i == NE - 1:
            emit_tap("ffn", x1f[:, 0:1024], [("cdv", vstt)])

    ffn_qi = [0]
    s_cur = [0]
    ffnq_owner = [0]
    exp_cnt = [0] * NE

    def ffn_drain(s, limit=None):
        n = 0
        so = ffnq_owner[0]
        while ffnq and (limit is None or n < limit):
            pick = None
            step = ffnq[0]
            c = step[-1]
            if c <= ffn_done_c[0]:
                pick = 0
            if pick is None:
                if limit is None:
                    raise RuntimeError("ffn_drain: no ready step on full drain")
                break
            step = ffnq.pop(pick)
            c = step[-1]
            qi = ffn_qi[0]
            ffn_qi[0] ^= 1
            if step[0] == "b":
                emit_ffn_bias(c, qi)
            else:
                _, e_i, c = step
                emit_ffn_exp(so, e_i, c, qi)
                exp_cnt[e_i] += 1
                if exp_cnt[e_i] == CPS:
                    free_and_feed(4 + e_i)
            n += 1

    def ln2_out(s):
        for c in range(CPS):
            g = 4 * s + c
            j = c % 2
            B.wait("DVE", "cdv", stt_val[c])
            B.op("DVE", lambda e, c=c: e.bn_stats(
                lnst2[:, 0:6], x1f[:, c * 1024: c * 1024 + 512]), inc="cdv")
            vb = B.op("DVE", lambda e, c=c: e.bn_stats(
                lnst2[:, 6:12], x1f[:, c * 1024 + 512: (c + 1) * 1024]),
                inc="cdv")
            B.wait("DVE", "cdv", vb)
            vagg = B.op("DVE", lambda e: e.bn_aggr(mnv2[:], lnst2[:]), inc="cdv")
            vrs = rsqrt_dve(mnv2[:, 1:2], rstd2[:], rsq[2],
                            [("cdv", vagg)])
            B.wait("DVE", "cdv", vrs)
            B.waits("DVE", osb_free[j])
            vo = B.op("DVE", lambda e, c=c, j=j: e.tensor_scalar(
                o_sb[j][:], x1f[:, c * 1024:(c + 1) * 1024],
                mnv2[:, 0:1], rstd2[:, 0:1], ALU.subtract, op1=ALU.mult),
                inc="cdv")
            stt_guard[c] = vo
            B.wait("POOL", "cdv", vo)
            vdma = dma("POOL", f"dos{j}", dout[g * 128:(g + 1) * 128, :],
                       o_sb[j][:])
            osb_free[j] = [(f"dos{j}", vdma)]

    nsc = DEBUG_SC if DEBUG_SC is not None else SC
    stage = DEBUG_STAGE if DEBUG_STAGE is not None else 99
    stage_holder[0] = stage
    x_stream(0)
    if stage >= 1:
        feed_weight(0)
        feed_weight(1)
    fed_b2 = [False]
    pend_ln2 = [None]
    for s in range(nsc if stage >= 1 else 0):
        s_cur[0] = s
        proj_core(s)
        if pend_ln2[0] is not None and stage >= 5:
            ffn_drain(pend_ln2[0])
            ln2_out(pend_ln2[0])
            pend_ln2[0] = None
        xtbf_free[0] = [("cpe", B.val("cpe"))]
        if s == 0:
            emit_tap("qt", qt_bf[:], [("cdv", qt_done[KB - 1])])
            emit_tap("kt", kt_bf[:], [("cdv", kt_done[KB - 1])])
            emit_tap("ktm", ktm_bf[:], [ktm_done[CPS - 1]])
            emit_tap("vext", vext_bf[:], [("cac", vext_done[CPS - 1])])
        prev_attn_evs = []
        if stage >= 4:
            ffn_done_c[0] = -1
            ffnq_owner[0] = s
            for cc in range(CPS):
                stt_val[cc] = None
                ffnq.append(("b", cc))
            for e_i in range(NE):
                exp_cnt[e_i] = 0
                for cc in range(CPS):
                    ffnq.append(("e", e_i, cc))
        if stage >= 2:
            scores_part(s, 0)
            for c in range(CPS):
                sz_part(s, c)
                if c > 0 and stage >= 3:
                    tail_chunk(s, c - 1, prev_attn_evs)
                prev_attn_evs = numden_div(s, c)
                if c < CPS - 1:
                    scores_part(s, c + 1)
                if c > 0 and stage >= 3:
                    tail_b(s, c - 1)
                    ffn_done_c[0] = c - 1
                    if stage >= 4:
                        ffn_drain(s, limit=3)
            if stage >= 3:
                tail_chunk(s, CPS - 1, prev_attn_evs)
                tail_b(s, CPS - 1)
                ffn_done_c[0] = CPS - 1
        if s + 1 < nsc:
            x_stream(s + 1)
        if s == 0 and stage >= 3:
            emit_tap("x1full", x1f[:], [("cdv", x1_apply[cc]) for cc in range(CPS)])
            emit_tap("wfull", w_all[:], [("cdv", w_done[cc]) for cc in range(CPS)])
        if stage >= 4:
            hold = 6 if s + 1 < nsc else 0
            ffn_drain(s, limit=max(0, len(ffnq) - hold))
        if s == 0 and not fed_b2[0]:
            wbuf_free[2] = list(land_free[0])
            feed_weight(2)
            fed_b2[0] = True
        if stage >= 5:
            if s + 1 < nsc:
                pend_ln2[0] = s
            else:
                ffn_drain(s)
                ln2_out(s)

    for cn in ("dxt0", "dxt1", "dx00", "dx01", "dos0", "dos1"):
        B.wait("POOL", cn, B.val(cn))
    for cn in ("dmc", "dwl0", "dwl1", "dwl2", "dscr", "dtap"):
        B.wait("SP", cn, B.val(cn))
    for cn in ("dwm0", "dwm1", "dwm2"):
        B.wait("POOL", cn, B.val(cn))

    with nc.Block() as block:
        @block.tensor
        def _(eng):
            for f in B.streams["PE"]:
                f(eng)

        @block.vector
        def _(eng):
            for f in B.streams["DVE"]:
                f(eng)

        @block.scalar
        def _(eng):
            for f in B.streams["ACT"]:
                f(eng)

        @block.gpsimd
        def _(eng):
            for f in B.streams["POOL"]:
                f(eng)

        @block.sync
        def _(eng):
            for f in B.streams["SP"]:
                f(eng)

    es.close()
    return nc


def _prep_shared(inputs):
    f = np.float32
    a = lambda v: np.ascontiguousarray(np.asarray(v, f))
    return {
        "wqT": a(np.asarray(inputs["L0_attn_Wq"], f).T),
        "wkT": a(np.asarray(inputs["L0_attn_Wk"], f).T),
        "wvT": a(np.asarray(inputs["L0_attn_Wv"], f).T),
        "woT": a(np.asarray(inputs["L0_attn_Wo"], f).T),
        "bq": a(inputs["L0_attn_bq"]).reshape(1, DM),
        "bk": a(inputs["L0_attn_bk"]).reshape(1, DM),
        "bv": a(inputs["L0_attn_bv"]).reshape(1, DM),
        "bo": a(inputs["L0_attn_bo"]).reshape(1, DM),
        "gwT": a(np.asarray(inputs["L0_ffn_gate_w"], f).T),
        "gb": a(inputs["L0_ffn_gate_b"]).reshape(1, NE),
        "ewT": a(np.asarray(inputs["L0_ffn_expert_w"], f).transpose(0, 2, 1)
                 ).reshape(NE * DM, DM),
        "eb": a(inputs["L0_ffn_expert_b"]).reshape(1, NE * DM),
        "ident": np.eye(128, dtype=f),
        "maskT": np.triu(np.ones((128, 128), dtype=f)),
        "pmask": np.stack([(np.arange(128) < 64).astype(f),
                           (np.arange(128) >= 64).astype(f)], axis=1),
        "bqc": a(inputs["L0_attn_bq"]).reshape(KB, 128).T,
        "bkc": a(inputs["L0_attn_bk"]).reshape(KB, 128).T,
    }


def kernel(**inputs):
    x = np.asarray(inputs["L0_attn_x"], np.float32)
    sh = _prep_shared(inputs)
    nc = build()
    in_maps = []
    for i in range(8):
        m = dict(sh)
        m["x"] = np.ascontiguousarray(x[i])
        m["xT"] = np.ascontiguousarray(x[i].T)
        in_maps.append(m)
    res = run_bass_kernel_spmd(nc, in_maps, core_ids=list(range(8)))
    return np.stack([res.results[i]["out"] for i in range(8)], axis=0)


# version 30 (final)
# speedup vs baseline: 1.0453x; 1.0453x over previous
import numpy as np
from contextlib import ExitStack

import concourse.bass as bass
import concourse.mybir as mybir
from concourse.bass_utils import run_bass_kernel_spmd

F32 = mybir.dt.float32
BF16 = mybir.dt.bfloat16
AF = mybir.ActivationFunctionType
ALU = mybir.AluOpType
AX = mybir.AxisListType

S = 4096
DM = 1024
NH = 16
HD = 64
CH = 128
SC = 8
TS = 512
CPS = 4
KB = 8
NE = 8
NMAT = 12
EPS = 1e-5

TAPS = {}
DEBUG_SC = None
DEBUG_STAGE = None
DEBUG_SCAN = None
DEBUG_R0ZERO = False
TAP_SPECS = {"qt": ([128, 4096], BF16), "kt": ([128, 4096], BF16),
             "ktm": ([128, 4096], BF16), "vext": ([128, 4160], BF16),
             "msc": ([128, 2048], BF16), "attn": ([128, 1024], BF16),
             "sbf": ([64, 1040], BF16), "x1": ([128, 1024], F32),
             "w": ([128, 8], F32), "ffn": ([128, 1024], F32),
             "x1full": ([128, 4096], F32), "wfull": ([128, 32], F32),
             "attn1": ([128, 1024], BF16), "msc1": ([128, 2048], BF16),
             "sbf1": ([64, 1040], BF16)}


class Bld:
    ENGS = ("PE", "DVE", "ACT", "POOL", "SP")

    def __init__(self, nc, es):
        self.nc = nc
        self.streams = {k: [] for k in self.ENGS}
        self.ctr = {}
        for name in ("cpe", "cdv", "cac", "cpl", "dmc", "dwl0", "dwl1",
                     "dwl2", "dwl3", "dwm0", "dwm1", "dwm2", "dwm3", "dscr",
                     "dxt0", "dxt1", "dx00", "dx01", "dos0", "dos1", "dtap"):
            h = es.enter_context(nc.semaphore(name))
            self.ctr[name] = [h, 0]
        self.waited = {e: {} for e in self.ENGS}

    def wait(self, eng, cname, val):
        if val is None or val <= 0:
            return
        w = self.waited[eng]
        if w.get(cname, 0) >= val:
            return
        w[cname] = val
        h = self.ctr[cname][0]
        self.streams[eng].append(lambda e, h=h, v=val: e.wait_ge(h, v))

    def waits(self, eng, evs):
        for cname, val in evs:
            self.wait(eng, cname, val)

    def op(self, eng, fn, inc=None, k=1):
        if inc is None:
            self.streams[eng].append(lambda e, fn=fn: fn(e))
            return None
        c = self.ctr[inc]
        c[1] += k
        v = c[1]
        h = c[0]

        def run(e, fn=fn, h=h, k=k):
            fn(e).then_inc(h, k)

        self.streams[eng].append(run)
        return v

    def val(self, cname):
        return self.ctr[cname][1]


def build():
    nc = bass.Bass("TRN2", target_bir_lowering=False)

    dx = nc.dram_tensor("x", [S, DM], F32, kind="ExternalInput").ap()
    dxT = nc.dram_tensor("xT", [DM, S], F32, kind="ExternalInput").ap()
    dwT = [nc.dram_tensor(n, [DM, DM], F32, kind="ExternalInput").ap()
           for n in ["wqT", "wkT", "wvT", "woT"]]
    dbias = {n: nc.dram_tensor(n, [1, DM], F32, kind="ExternalInput").ap()
             for n in ["bq", "bk", "bv", "bo"]}
    dgwT = nc.dram_tensor("gwT", [DM, NE], F32, kind="ExternalInput").ap()
    dgb = nc.dram_tensor("gb", [1, NE], F32, kind="ExternalInput").ap()
    dewT = nc.dram_tensor("ewT", [NE * DM, DM], F32, kind="ExternalInput").ap()
    deb = nc.dram_tensor("eb", [1, NE * DM], F32, kind="ExternalInput").ap()
    dident = nc.dram_tensor("ident", [128, 128], F32, kind="ExternalInput").ap()
    dmaskT = nc.dram_tensor("maskT", [128, 128], F32, kind="ExternalInput").ap()
    dpmask = nc.dram_tensor("pmask", [128, 2], F32, kind="ExternalInput").ap()
    dbqc = nc.dram_tensor("bqc", [128, KB], F32, kind="ExternalInput").ap()
    dbkc = nc.dram_tensor("bkc", [128, KB], F32, kind="ExternalInput").ap()
    dout = nc.dram_tensor("out", [S, DM], F32, kind="ExternalOutput").ap()
    descr = nc.dram_tensor("escr", [NMAT * 128, 8192], BF16, kind="Internal").ap()
    dtaps = {n: nc.dram_tensor("tap_" + n, *TAP_SPECS[n], kind="ExternalOutput").ap()
             for n in TAPS if n in TAP_SPECS}

    es = ExitStack()
    sb = lambda name, shape, dt: es.enter_context(nc.sbuf_tensor(name, shape, dt))
    psal = lambda name, shape, dt: es.enter_context(nc.psum_tensor(name, shape, dt))

    wb = [sb(f"wb{j}", [128, 8192], BF16) for j in range(4)]
    b2f = wb[2][:].bitcast(F32)
    xt_land = [sb(f"xtl{j}", [128, 512], F32) for j in range(2)]
    xt_bf = sb("xt_bf", [128, 4096], BF16)
    x0c = [sb(f"x0c{j}", [128, 1024], F32) for j in range(2)]
    qt_bf = sb("qt_bf", [128, 4096], BF16)
    kt_bf = sb("kt_bf", [128, 4096], BF16)
    ktm_bf = sb("ktm_bf", [128, 4096], BF16)
    vext_bf = sb("vext_bf", [128, 4160], BF16)
    msc_bf = sb("msc_bf", [128, 2048], BF16)
    attn_bfp = [sb(f"attnb{j}", [128, 1024], BF16) for j in range(2)]
    attnT_bfp = [sb(f"attnTb{j}", [128, 1024], BF16) for j in range(2)]
    x1f = sb("x1f", [128, 4096], F32)
    x1c_bf = sb("x1c_bf", [128, 1024], BF16)
    x1t_bf = sb("x1t_bf", [128, 4096], BF16)
    o_sb = [sb(f"osb{j}", [128, 1024], F32) for j in range(2)]
    S_f = sb("S_f", [64, 1040], F32)
    S_bfp = [sb(f"Sbf{j}", [128, 1040], BF16) for j in range(2)]
    e_tmp = [sb(f"etmp{j}", [128, 512], F32) for j in range(2)]
    e_tmp2 = [sb(f"etmp2{j}", [128, 512], F32) for j in range(2)]
    id_f = sb("id_f", [128, 128], F32)
    id_bf = sb("id_bf", [128, 128], BF16)
    id_f32 = sb("id_f32", [128, 128], F32)
    maskT_f = sb("maskT_f", [128, 128], F32)
    pmask = sb("pmask_s", [128, 2], F32)
    bqc_s = sb("bqc_s", [128, KB], F32)
    bkc_s = sb("bkc_s", [128, KB], F32)
    wT_all = sb("wT_all", [8, 512], BF16)
    w_bfs = sb("w_bfs", [128, 32], BF16)
    ebias8 = sb("ebias8", [8, 1024], BF16)
    ktz = sb("ktz", [128, 2048], BF16)
    qtz = sb("qtz", [128, 2048], BF16)
    ones_r = sb("ones_r", [1, 512], F32)
    bias_s = {n: sb(n + "_s", [1, 1024], F32) for n in ["bv", "bo"]}
    gb_s = sb("gb_s", [1, NE], F32)
    gwT_f = sb("gwT_f", [128, 64], F32)
    gwT_b = sb("gwT_b", [128, 64], BF16)
    lnst = [sb(f"lnst{j}", [128, 12], F32) for j in range(2)]
    mnv = [sb(f"mnv{j}", [128, 2], F32) for j in range(2)]
    sdev = [sb(f"sdev{j}", [128, 1], F32) for j in range(2)]
    rstd = [sb(f"rstd{j}", [128, 1], F32) for j in range(2)]
    lnst2 = sb("lnst2", [128, 12], F32)
    mnv2 = sb("mnv2", [128, 2], F32)
    sdev2 = sb("sdev2", [128, 1], F32)
    rstd2 = sb("rstd2", [128, 1], F32)
    negm = [sb(f"negm{j}", [128, 1], F32) for j in range(2)]
    eg = [sb(f"eg{j}", [128, NE], F32) for j in range(2)]
    sg = [sb(f"sg{j}", [128, 1], F32) for j in range(2)]
    rg = [sb(f"rg{j}", [128, 1], F32) for j in range(2)]
    gt = [sb(f"gt{j}", [128, NE], F32) for j in range(2)]
    t8 = [sb(f"t8{j}", [128, NE], F32) for j in range(2)]
    w_all = sb("w_all", [128, 32], F32)
    rden = sb("rden", [128, 16], F32)
    eps_c = sb("eps_c", [128, 1], F32)
    magic_c = sb("magic_c", [128, 1], mybir.dt.int32)
    rsq = [sb(f"rsq{j}", [128, 6], F32) for j in range(3)]

    quad = [psal(f"q{i}", [128, 1024], F32) for i in range(4)]
    qbf = [q[:].bitcast(BF16) for q in quad]

    B = Bld(nc, es)

    wready = [[] for _ in range(NMAT)]
    wpieces = [[[] for _ in range(KB)] for _ in range(NMAT)]
    wslot = [None] * NMAT
    wbuf_free = [[], [], [], []]
    slot_store = [[], [], [], []]
    store_val = [0] * NMAT
    qfree = {}

    def qwait(qi, halves=(0, 1)):
        evs = []
        for hh in halves:
            evs += qfree.get((qi, hh), [])
        return evs

    def qset(qi, halves, evs):
        for hh in halves:
            qfree[(qi, hh)] = list(evs)

    def dma(eng, cname, out_ap, in_ap):
        return B.op(eng, lambda e, o=out_ap, i=in_ap: e.dma_start(o, i),
                    inc=cname, k=16)

    dma("SP", "dmc", id_f[:], dident)
    dma("SP", "dmc", maskT_f[:], dmaskT)
    dma("SP", "dmc", pmask[:], dpmask)
    dma("SP", "dmc", bqc_s[:], dbqc)
    dma("SP", "dmc", bkc_s[:], dbkc)
    for n in ["bv", "bo"]:
        dma("SP", "dmc", bias_s[n][:], dbias[n])
    dma("SP", "dmc", gb_s[:], dgb)
    dma("SP", "dmc", gwT_f[:].rearrange("p (k e) -> p k e", e=NE),
        dgwT.rearrange("(k p) e -> p k e", p=128))
    v_cdone = B.val("dmc")
    v_eb0 = dma("SP", "dwl2", b2f[0:8, 0:1024],
                deb.rearrange("o (e h) -> (o e) h", h=DM))

    B.op("POOL", lambda e: e.memset(eps_c[:], EPS), inc="cpl")
    B.op("POOL", lambda e: e.memset(magic_c[:], 0x5F3759DF), inc="cpl")
    B.op("POOL", lambda e: e.memset(ones_r[:], 1.0), inc="cpl")
    B.op("POOL", lambda e: e.memset(vext_bf[:], 1.0), inc="cpl")
    B.op("POOL", lambda e: e.memset(S_f[:], 0.0), inc="cpl")
    B.op("POOL", lambda e: e.memset(S_bfp[0][:], 0.0), inc="cpl")
    v_ms = B.op("POOL", lambda e: e.memset(S_bfp[1][:], 0.0), inc="cpl")

    B.wait("DVE", "dmc", v_cdone)
    v_idb = B.op("DVE", lambda e: e.tensor_copy(id_bf[:], id_f[:]), inc="cdv")
    v_idf = B.op("DVE", lambda e: e.tensor_copy(id_f32[:], id_f[:]), inc="cdv")
    v_gwb = B.op("DVE", lambda e: e.tensor_copy(gwT_b[:], gwT_f[:]), inc="cdv")
    B.wait("DVE", "dwl2", v_eb0)
    v_ebc1 = B.op("DVE", lambda e: e.tensor_copy(ebias8[:], b2f[0:8, 0:1024]),
                  inc="cdv")

    WSCHED = [(s, m) for s in range(SC) for m in range(NMAT)]
    wptr = [0]
    land_free = [[("cdv", v_ebc1)]]

    def w_src_half(m, q):
        if m < 4:
            return dwT[m][q * 512:(q + 1) * 512, :].rearrange(
                "(k p) o -> p k o", p=128)
        e = m - 4
        return dewT[e * DM + q * 512: e * DM + (q + 1) * 512, :].rearrange(
            "(k p) o -> p k o", p=128)

    def feed_weight(slot):
        if wptr[0] >= len(WSCHED):
            return
        s, m = WSCHED[wptr[0]]
        wptr[0] += 1
        dst = wb[slot]
        if s == 0:
            ws = []
            for q in range(2):
                B.waits("SP", land_free[0])
                vldw = dma("SP", "dwl2",
                           b2f[:].rearrange("p (k o) -> p k o", o=1024),
                           w_src_half(m, q))
                eng = "DVE" if (m + q) % 2 == 0 else "ACT"
                cn = "cdv" if eng == "DVE" else "cac"
                B.wait(eng, "dwl2", vldw)
                B.waits(eng, wbuf_free[slot])
                if eng == "DVE":
                    vc = B.op(eng, lambda e, d=dst, q=q: e.tensor_copy(
                        d[:, q * 4096:(q + 1) * 4096], b2f[:]), inc=cn)
                else:
                    vc = B.op(eng, lambda e, d=dst, q=q: e.activation(
                        d[:, q * 4096:(q + 1) * 4096], b2f[:], AF.Copy), inc=cn)
                ws.append((cn, vc))
                land_free[0] = [(cn, vc)]
            wready[m] = list(ws)
            for k in range(KB):
                wpieces[m][k] = [ws[k // 4]]
            wslot[m] = slot
            B.waits("SP", ws)
            B.wait("SP", "dscr", B.val("dscr"))
            store_val[m] = dma("SP", "dscr", descr[m * 128:(m + 1) * 128, :], dst[:])
            slot_store[slot] = [("dscr", store_val[m])]
        else:
            B.waits("SP", wbuf_free[slot])
            B.wait("SP", "dscr", store_val[m])
            B.waits("POOL", wbuf_free[slot])
            B.wait("POOL", "dscr", store_val[m])
            if m < 2:
                evs = []
                for k in range(KB):
                    eng = "SP" if k % 2 == 0 else "POOL"
                    cn = (f"dwl{slot}" if k % 2 == 0 else f"dwm{slot}")
                    B.wait(eng, cn, B.val(cn))
                    v = dma(eng, cn, wb[slot][:, k * 1024:(k + 1) * 1024],
                            descr[m * 128:(m + 1) * 128,
                                  k * 1024:(k + 1) * 1024])
                    wpieces[m][k] = [(cn, v)]
                    evs.append((cn, v))
                wready[m] = [evs[-1], evs[-2]]
            else:
                v1 = dma("SP", f"dwl{slot}", wb[slot][:, 0:4096],
                         descr[m * 128:(m + 1) * 128, 0:4096])
                v2 = dma("POOL", f"dwm{slot}", wb[slot][:, 4096:8192],
                         descr[m * 128:(m + 1) * 128, 4096:8192])
                wready[m] = [(f"dwl{slot}", v1), (f"dwm{slot}", v2)]
                for k in range(KB):
                    wpieces[m][k] = wready[m]
            wslot[m] = slot

    def free_and_feed(mat):
        slot = wslot[mat]
        done = B.val("cpe")
        wbuf_free[slot] = [("cpe", done)] + slot_store[slot]
        slot_store[slot] = []
        feed_weight(slot)

    xt_cast = [0] * KB
    x1c_free = [[]]
    xtl_free = [[], []]
    xtbf_free = [[]]
    x0_dma = [0, 0]
    x0_free = [[], []]
    osb_free = [[], []]
    qt_done = [0] * KB
    kt_done = [0] * KB
    ktm_done = [("cac", 0)] * CPS
    vext_done = [0] * CPS
    etmp_free = [[], []]
    etmp2_free = [[], []]
    sbf_cast = {-1: ("cpl", v_ms)}
    sbf_cast2 = {-1: ("cpl", v_ms)}
    msc_done = [0, 0]
    ktz_free = [[]]
    qtz_free = [[]]
    kqz_done = [(None, 0, 0)]
    div_done = [0, 0]
    x1t_evac = [0] * CPS
    x1_apply = [0] * CPS
    w_done = [0] * CPS
    w_bf_done = [0] * CPS
    wT_done = [0] * CPS
    stt_val = [None] * CPS
    stt_guard = [0] * CPS
    vxb_prev = [0]
    tail_state = {}

    tap_guard = [0]
    stage_holder = [99]

    def emit_tap(name, src_ap, evs):
        if name not in dtaps:
            return
        B.waits("SP", evs)
        B.wait("SP", "dtap", B.val("dtap"))
        tap_guard[0] = dma("SP", "dtap", dtaps[name], src_ap)

    def x_stream(s):
        for engx in ("PE", "DVE", "ACT", "POOL"):
            B.wait(engx, "dtap", tap_guard[0])
        B.wait("PE", "cpl", v_ms)
        B.wait("PE", "dmc", v_cdone)
        B.wait("ACT", "cpl", v_ms)
        B.wait("DVE", "cpl", v_ms)
        for k in range(KB):
            j = k % 2
            B.waits("POOL", xtl_free[j])
            vd = dma("POOL", f"dxt{j}", xt_land[j][:],
                     dxT[k * 128:(k + 1) * 128, s * 512:(s + 1) * 512])
            B.wait("DVE", f"dxt{j}", vd)
            B.waits("DVE", xtbf_free[0])
            vc = B.op("DVE", lambda e, k=k, j=j: e.tensor_copy(
                xt_bf[:, k * 512:(k + 1) * 512], xt_land[j][:]), inc="cdv")
            xt_cast[k] = vc
            xtl_free[j] = [("cdv", vc)]

    def proj_core(s):
        def proj_qk(mat, dstbf, donearr, bias_col):
            src = wb[wslot[mat]]
            vpe_f = [None] * KB
            for wv in range(2):
                fs = range(wv * 4, wv * 4 + 4)
                for k in range(KB):
                    B.waits("PE", wpieces[mat][k])
                    B.wait("PE", "cdv", xt_cast[k])
                    if k == 0:
                        for f in fs:
                            B.waits("PE", qwait(2 + (f % 4) // 2, ((f % 4) % 2,)))
                    for f in fs:
                        fn = lambda e, src=src, k=k, f=f: e.matmul(
                            quad[2 + (f % 4) // 2][
                                :, ((f % 4) % 2) * 512:((f % 4) % 2 + 1) * 512],
                            src[:, k * 1024 + f * 128: k * 1024 + (f + 1) * 128],
                            xt_bf[:, k * 512:(k + 1) * 512],
                            start=(k == 0), stop=(k == KB - 1))
                        if k == KB - 1:
                            vpe_f[f] = B.op("PE", fn, inc="cpe")
                        else:
                            B.op("PE", fn)
                for f in fs:
                    emit_phi(mat, dstbf, donearr, bias_col, f, vpe_f[f])
                ffn_drain(s_cur[0], limit=2)
            free_and_feed(mat)

        def emit_phi(mat, dstbf, donearr, bias_col, f, vpe):
            if True:
                qi, hh = 2 + (f % 4) // 2, (f % 4) % 2
                et = e_tmp[f % 2]
                et2 = e_tmp2[f % 2]
                bc = bias_col[:, f:f + 1]
                B.wait("ACT", "cpe", vpe)
                B.waits("ACT", etmp_free[f % 2])
                va = B.op("ACT", lambda e, qi=qi, hh=hh, et=et, bc=bc: e.activation(
                    et[:], quad[qi][:, hh * 512:(hh + 1) * 512], AF.Exp,
                    bias=bc), inc="cac")
                B.waits("ACT", etmp2_free[f % 2])
                va2 = B.op("ACT", lambda e, qi=qi, hh=hh, et2=et2, bc=bc:
                           e.activation(
                    et2[:], quad[qi][:, hh * 512:(hh + 1) * 512], AF.Relu,
                    bias=bc), inc="cac")
                qset(qi, (hh,), [("cac", va2)])
                B.wait("DVE", "cac", va2)
                vd = B.op("DVE", lambda e, et=et, et2=et2, f=f, d=dstbf:
                          e.scalar_tensor_tensor(
                              d[:, f * 512:(f + 1) * 512],
                              et[:], 1.0, et2[:], ALU.min, ALU.add), inc="cdv")
                donearr[f] = vd
                etmp_free[f % 2] = [("cdv", vd)]
                etmp2_free[f % 2] = [("cdv", vd)]

        proj_qk(0, qt_bf, qt_done, bqc_s)
        proj_qk(1, kt_bf, kt_done, bkc_s)
        if (DEBUG_SCAN or 99) >= 1 and stage_holder[0] >= 2:
            build_zmask(0)
            build_qz(0)

        for c in range(CPS):
            qi = c % 2
            B.waits("PE", qwait(qi))
            B.waits("PE", wready[2])
            src = wb[wslot[2]]
            vpe = None
            for k in range(KB):
                for hh in range(2):
                    B.op("PE", lambda e, src=src, k=k, c=c, qi=qi, hh=hh: e.matmul(
                        quad[qi][:, hh * 512:(hh + 1) * 512],
                        xt_bf[:, k * 512 + c * 128: k * 512 + (c + 1) * 128],
                        src[:, k * 1024 + hh * 512: k * 1024 + (hh + 1) * 512],
                        start=(k == 0), stop=False))
            for hh in range(2):
                fn = lambda e, qi=qi, hh=hh: e.matmul(
                    quad[qi][:, hh * 512:(hh + 1) * 512],
                    ones_r[0:1, 0:128],
                    bias_s["bv"][0:1, hh * 512:(hh + 1) * 512],
                    start=False, stop=True)
                if hh == 0:
                    B.op("PE", fn)
                else:
                    vpe = B.op("PE", fn, inc="cpe")
            B.wait("ACT", "cpe", vpe)
            va = B.op("ACT", lambda e, c=c, qi=qi: e.activation(
                vext_bf[:, c * 1040:(c + 1) * 1040].rearrange(
                    "p (h w) -> p h w", w=65)[:, :, 0:64],
                quad[qi][:].rearrange("p (h w) -> p h w", w=64),
                AF.Copy), inc="cac")
            vext_done[c] = va
            qset(qi, (0, 1), [("cac", va)])
        free_and_feed(2)

        B.wait("PE", "cdv", v_idb)
        for c in range(CPS):
            qi = (c + 2) % 4
            B.waits("PE", qwait(qi))
            vpe = None
            for f in range(KB):
                B.wait("PE", "cdv", kt_done[f])
                fn = lambda e, c=c, f=f, qi=qi: e.matmul(
                    quad[qi][:, f * 128:(f + 1) * 128],
                    kt_bf[:, f * 512 + c * 128: f * 512 + (c + 1) * 128],
                    id_bf[:], start=True, stop=True)
                if f < KB - 1:
                    B.op("PE", fn)
                else:
                    vpe = B.op("PE", fn, inc="cpe")
            B.wait("DVE", "cpe", vpe)
            vd = B.op("DVE", lambda e, c=c, qi=qi: e.tensor_copy(
                ktm_bf[:, c * 1024:(c + 1) * 1024], quad[qi][:]), inc="cdv")
            ktm_done[c] = ("cdv", vd)
            qset(qi, (0, 1), [("cdv", vd)])

    def rsqrt_dve(var_ap, out_ap, scr, wait_ev):
        B.waits("DVE", wait_ev)
        ve = scr[:, 0:1]
        vh = scr[:, 1:2]
        ya = scr[:, 2:3]
        yb = scr[:, 3:4]
        yc = scr[:, 4:5]
        yy = scr[:, 5:6]
        v = B.op("DVE", lambda e: e.tensor_scalar(
            ve, var_ap, eps_c[:, 0:1], None, ALU.add), inc="cdv")
        B.wait("DVE", "cdv", v)
        v = B.op("DVE", lambda e: e.tensor_scalar(
            vh, ve, 0.5, None, ALU.mult), inc="cdv")
        B.wait("DVE", "cdv", v)
        v = B.op("DVE", lambda e: e.tensor_scalar(
            yb.bitcast(mybir.dt.int32), ve.bitcast(mybir.dt.int32), 1, None,
            ALU.arith_shift_right), inc="cdv")
        B.wait("DVE", "cdv", v)
        v = B.op("DVE", lambda e: e.tensor_tensor(
            yy.bitcast(mybir.dt.int32), magic_c[:, 0:1],
            yb.bitcast(mybir.dt.int32), ALU.subtract), inc="cdv")
        for _ in range(2):
            B.wait("DVE", "cdv", v)
            v = B.op("DVE", lambda e: e.tensor_tensor(ya, yy, yy, ALU.mult),
                     inc="cdv")
            B.wait("DVE", "cdv", v)
            v = B.op("DVE", lambda e: e.tensor_tensor(yb, ya, vh, ALU.mult),
                     inc="cdv")
            B.wait("DVE", "cdv", v)
            v = B.op("DVE", lambda e: e.tensor_scalar(
                yc, yb, 1.5, -1.0, ALU.subtract, op1=ALU.mult), inc="cdv")
            B.wait("DVE", "cdv", v)
            v = B.op("DVE", lambda e: e.tensor_tensor(yy, yy, yc, ALU.mult),
                     inc="cdv")
        B.wait("DVE", "cdv", v)
        v = B.op("DVE", lambda e: e.tensor_copy(out_ap, yy), inc="cdv")
        return v

    def build_zmask(c):
        B.wait("ACT", "cdv", max(kt_done))
        B.waits("ACT", ktz_free[0])
        B.wait("ACT", "dmc", v_cdone)
        vkz = None
        for par in range(2):
            mcol = pmask[:, par:par + 1]
            vkz = B.op("ACT", lambda e, c=c, par=par, mcol=mcol: e.activation(
                ktz[:].rearrange("p (h w) -> p h w", w=128)[:, par::2, :],
                kt_bf[:].rearrange("p (f w) -> p f w", w=512)[:, :, c * 128:(c + 1) * 128],
                AF.Copy, scale=mcol), inc="cac")
        kqz_done[0] = ("ACT", vkz, kqz_done[0][2])

    def build_qz(c):
        B.wait("DVE", "cdv", max(qt_done))
        B.waits("DVE", qtz_free[0])
        B.wait("DVE", "dmc", v_cdone)
        vqz = None
        for par in range(2):
            mcol = pmask[:, par:par + 1]
            vqz = B.op("DVE", lambda e, c=c, par=par, mcol=mcol: e.tensor_scalar(
                qtz[:].rearrange("p (h w) -> p h w", w=128)[:, par::2, :],
                qt_bf[:].rearrange("p (f w) -> p f w", w=512)[:, :, c * 128:(c + 1) * 128],
                mcol, None, ALU.mult), inc="cdv")
        kqz_done[0] = (kqz_done[0][0], kqz_done[0][1], vqz)

    def scores_part(s, c):
        g = 4 * s + c
        dbg = DEBUG_SCAN or 99
        if dbg == 6:
            return
        for half in range(2):
            qi = half if dbg == 8 else 2 + half
            B.waits("PE", qwait(qi))
            B.wait("DVE", "dtap", tap_guard[0])
            B.wait("PE", "cac", kqz_done[0][1])
            B.wait("PE", "cdv", kqz_done[0][2])
            vsc = None
            for hh in range(8):
                h = half * 8 + hh
                f = h // 2
                B.wait("PE", "cdv", qt_done[f])
                fn = lambda e, h=h, f=f, c=c, qi=qi, hh=hh: e.matmul(
                    quad[qi][:, hh * 128:(hh + 1) * 128],
                    ktz[:, h * 128:(h + 1) * 128],
                    qt_bf[:, f * 512 + c * 128: f * 512 + (c + 1) * 128],
                    start=True, stop=True)
                if hh < 7:
                    B.op("PE", fn)
                else:
                    vsc = B.op("PE", fn, inc="cpe")
            B.wait("DVE", "cpe", vsc)
            B.wait("DVE", "dmc", v_cdone)
            if dbg == 4:
                vm = B.op("DVE", lambda e, half=half, qi=qi: e.tensor_copy(
                    msc_bf[:, half * 1024:(half + 1) * 1024], quad[qi][:]),
                    inc="cdv")
            elif dbg in (7, 8, 9):
                vm = B.op("DVE", lambda e, half=half: e.tensor_copy(
                    msc_bf[:, half * 1024:(half + 1) * 1024],
                    kt_bf[:, 0:1024]), inc="cdv")
            else:
                vm = B.op("DVE", lambda e, half=half, qi=qi: e.tensor_tensor(
                    msc_bf[:, half * 1024:(half + 1) * 1024].rearrange(
                        "p (h w) -> p h w", w=128),
                    quad[qi][:].rearrange("p (h w) -> p h w", w=128),
                    maskT_f[:].unsqueeze(1).broadcast_to([128, 8, 128]),
                    ALU.mult), inc="cdv")
            msc_done[half] = vm
        ktz_free[0] = [("cpe", B.val("cpe"))]
        if c < CPS - 1:
            build_zmask(c + 1)
        if s == 0 and c == 0:
            emit_tap("msc", msc_bf[:], [("cdv", msc_done[0]), ("cdv", msc_done[1])])
        if s == 0 and c == 1:
            emit_tap("msc1", msc_bf[:], [("cdv", msc_done[0]), ("cdv", msc_done[1])])

    def sz_part(s, c):
        g = 4 * s + c
        dbg = DEBUG_SCAN or 99
        j = c % 2
        if dbg not in (4, 5, 7, 8, 9):
            B.waits("POOL", x0_free[j])
            x0_dma[j] = dma("POOL", f"dx0{j}", x0c[j][:],
                            dx[g * 128:(g + 1) * 128, :])
        if dbg < 2 or dbg in (4, 5, 6, 7, 8, 9):
            sbf_cast[g] = sbf_cast[g - 1]
            return
        B.waits("PE", qwait(1))
        B.waits("PE", qwait(0, (0,)))
        B.wait("PE", "cac", vext_done[c])
        B.wait("PE", ktm_done[c][0], ktm_done[c][1])
        for h in range(NH):
            B.op("PE", lambda e, h=h, c=c: e.matmul(
                quad[1][0:64, h * 64:(h + 1) * 64],
                ktm_bf[:, c * 1024 + h * 64: c * 1024 + (h + 1) * 64],
                vext_bf[:, c * 1040 + h * 65: c * 1040 + h * 65 + 64],
                start=True, stop=True))
        vsz = None
        for h in range(NH):
            fn = lambda e, h=h, c=c: e.matmul(
                quad[0][0:64, h:h + 1],
                ktm_bf[:, c * 1024 + h * 64: c * 1024 + (h + 1) * 64],
                vext_bf[:, c * 1040 + h * 65 + 64: c * 1040 + h * 65 + 65],
                start=True, stop=True)
            if h < NH - 1:
                B.op("PE", fn)
            else:
                vsz = B.op("PE", fn, inc="cpe")
        B.wait("DVE", "cpe", vsz)
        cnp, vvp = sbf_cast[g - 1]
        B.wait("DVE", cnp, vvp)
        cnp2, vvp2 = sbf_cast2[g - 1]
        B.wait("DVE", cnp2, vvp2)
        B.op("DVE", lambda e: e.tensor_tensor(
            S_f[:].rearrange("p (h w) -> p h w", w=65)[:, :, 0:64],
            quad[1][0:64, :].rearrange("p (h w) -> p h w", w=64),
            S_f[:].rearrange("p (h w) -> p h w", w=65)[:, :, 0:64],
            ALU.add), inc="cdv")
        vsadd = B.op("DVE", lambda e: e.tensor_tensor(
            S_f[:].rearrange("p (h w) -> p h w", w=65)[:, :, 64:65],
            quad[0][0:64, 0:16].unsqueeze(2),
            S_f[:].rearrange("p (h w) -> p h w", w=65)[:, :, 64:65],
            ALU.add), inc="cdv")
        qset(1, (0, 1), [("cdv", vsadd)])
        qset(0, (0,), [("cdv", vsadd)])
        B.wait("ACT", "cdv", vsadd)
        B.wait("ACT", "dtap", tap_guard[0])
        B.wait("DVE", "dtap", tap_guard[0])
        vsc1 = B.op("ACT", lambda e, g=g: e.activation(
            S_bfp[g % 2][0:64, :], S_f[:], AF.Copy), inc="cac")
        B.wait("DVE", "cdv", vsadd)
        vscast = B.op("DVE", lambda e, g=g: e.tensor_copy(
            S_bfp[g % 2][64:128, :], S_f[:]), inc="cdv")
        sbf_cast[g] = ("cdv", vscast)
        sbf_cast2[g] = ("cac", vsc1)
        if s == 0 and c == 0:
            emit_tap("sbf", S_bfp[0][0:64, :], [("cac", vsc1), ("cdv", vscast)])
        if s == 0 and c == 1:
            emit_tap("sbf1", S_bfp[1][0:64, :], [("cac", vsc1), ("cdv", vscast)])

    def numden_div(s, c):
        if (DEBUG_SCAN or 99) < 3 or (DEBUG_SCAN or 99) in (4, 5, 6, 7, 8, 9):
            return []
        g = 4 * s + c
        out_evs = []
        for half in range(2):
            qi = 2 + half
            B.wait("PE", "cdv", msc_done[half])
            cn, vv = sbf_cast[g - 1]
            B.wait("PE", cn, vv)
            cn2, vv2 = sbf_cast2[g - 1]
            B.wait("PE", cn2, vv2)
            vnd = None
            B.wait("PE", "cdv", kqz_done[0][2])
            B.wait("PE", "cac", kqz_done[0][1])
            for hh in range(8):
                h = half * 8 + hh
                B.op("PE", lambda e, h=h, hh=hh, c=c, qi=qi: e.matmul(
                    quad[qi][:, hh * 128: hh * 128 + 65],
                    msc_bf[:, h * 128:(h + 1) * 128],
                    vext_bf[:, c * 1040 + h * 65: c * 1040 + (h + 1) * 65],
                    start=True, stop=False))
                fn = lambda e, h=h, hh=hh, c=c, qi=qi, g=g: e.matmul(
                    quad[qi][:, hh * 128: hh * 128 + 65],
                    qtz[:, h * 128:(h + 1) * 128],
                    S_bfp[(g - 1) % 2][:, h * 65:(h + 1) * 65],
                    start=False, stop=True)
                if hh < 7:
                    B.op("PE", fn)
                else:
                    vnd = B.op("PE", fn, inc="cpe")
            B.wait("DVE", "cpe", vnd)
            B.wait("DVE", "dtap", tap_guard[0])
            vr = B.op("DVE", lambda e, half=half, qi=qi: e.reciprocal(
                rden[:, half * 8:(half + 1) * 8].unsqueeze(2),
                quad[qi][:].rearrange("p (h w) -> p h w", w=128)[:, :, 64:65]),
                inc="cdv")
            B.wait("DVE", "cdv", vr)
            vd = B.op("DVE", lambda e, half=half, qi=qi, c=c: e.tensor_tensor(
                attn_bfp[c % 2][:, half * 512:(half + 1) * 512].rearrange(
                    "p (h w) -> p h w", w=64),
                quad[qi][:].rearrange("p (h w) -> p h w", w=128)[:, :, 0:64],
                rden[:, half * 8:(half + 1) * 8].unsqueeze(2).broadcast_to(
                    [128, 8, 64]),
                ALU.mult), inc="cdv")
            div_done[half] = vd
            qset(qi, (0, 1), [("cdv", vd)])
            out_evs.append(("cdv", vd))
        qtz_free[0] = [("cpe", B.val("cpe"))]
        if c < CPS - 1:
            build_qz(c + 1)
        if s == 0 and c == 0:
            emit_tap("attn", attn_bfp[0][:], out_evs)
        if s == 0 and c == 1:
            emit_tap("attn1", attn_bfp[1][:], out_evs)
        return out_evs

    def tail_chunk(s, c, attn_evs):
        g = 4 * s + c
        j = c % 2
        B.waits("PE", qwait(0))
        B.waits("PE", attn_evs)
        B.wait("PE", "cdv", v_idb)
        vat = None
        for f in range(KB):
            fn = lambda e, f=f, j=j: e.matmul(
                quad[0][:, f * 128:(f + 1) * 128],
                attn_bfp[j][:, f * 128:(f + 1) * 128], id_bf[:],
                start=True, stop=True)
            if f < KB - 1:
                B.op("PE", fn)
            else:
                vat = B.op("PE", fn, inc="cpe")
        B.wait("ACT", "cpe", vat)
        vae = B.op("ACT", lambda e, j=j: e.activation(
            attnT_bfp[j][:], quad[0][:], AF.Copy), inc="cac")
        B.wait("PE", "cac", vae)
        B.waits("PE", qwait(0, (0,)))
        B.waits("PE", wready[3])
        src = wb[wslot[3]]
        vP = None
        for k in range(KB):
            for hh in range(2):
                B.op("PE", lambda e, src=src, k=k, hh=hh, j=j: e.matmul(
                    quad[0][:, hh * 512:(hh + 1) * 512],
                    attnT_bfp[j][:, k * 128:(k + 1) * 128],
                    src[:, k * 1024 + hh * 512: k * 1024 + (hh + 1) * 512],
                    start=(k == 0), stop=False))
        for hh in range(2):
            fn = lambda e, hh=hh: e.matmul(
                quad[0][:, hh * 512:(hh + 1) * 512],
                ones_r[0:1, 0:128],
                bias_s["bo"][0:1, hh * 512:(hh + 1) * 512],
                start=False, stop=True)
            if hh == 0:
                B.op("PE", fn)
            else:
                vP = B.op("PE", fn, inc="cpe")
        if c == CPS - 1:
            free_and_feed(3)
        B.wait("DVE", "cpe", vP)
        B.wait("DVE", f"dx0{j}", x0_dma[j])
        B.wait("DVE", "cdv", stt_guard[c])
        vres = B.op("DVE", lambda e, j=j, c=c: e.tensor_tensor(
            x1f[:, c * 1024:(c + 1) * 1024], quad[0][:], x0c[j][:], ALU.add),
            inc="cdv")
        x0_free[j] = [("cdv", vres)]
        qset(0, (0, 1), [("cdv", vres)])
        B.wait("DVE", "cdv", vres)
        B.op("DVE", lambda e, j=j, c=c: e.bn_stats(
            lnst[j][:, 0:6], x1f[:, c * 1024: c * 1024 + 512]), inc="cdv")
        vb = B.op("DVE", lambda e, j=j, c=c: e.bn_stats(
            lnst[j][:, 6:12], x1f[:, c * 1024 + 512:(c + 1) * 1024]), inc="cdv")
        B.wait("DVE", "cdv", vb)
        vagg = B.op("DVE", lambda e, j=j: e.bn_aggr(mnv[j][:], lnst[j][:]),
                    inc="cdv")
        B.wait("ACT", "cdv", vagg)
        vsq = B.op("ACT", lambda e, j=j: e.activation(
            sdev[j][:], mnv[j][:, 1:2], AF.Sqrt, bias=eps_c[:, 0:1]), inc="cac")
        B.wait("DVE", "cac", vsq)
        vrs = B.op("DVE", lambda e, j=j: e.reciprocal(rstd[j][:], sdev[j][:]),
                   inc="cdv")
        B.wait("DVE", "cdv", vrs)
        vx1 = B.op("DVE", lambda e, c=c, j=j: e.tensor_scalar(
            x1f[:, c * 1024:(c + 1) * 1024], x1f[:, c * 1024:(c + 1) * 1024],
            mnv[j][:, 0:1], rstd[j][:, 0:1], ALU.subtract, op1=ALU.mult),
            inc="cdv")
        x1_apply[c] = vx1
        if s == 0 and c == 0:
            emit_tap("x1", x1f[:, 0:1024], [("cdv", vx1)])
        tail_state[c] = vx1

    def tail_b(s, c):
        vx1 = tail_state[c]
        j = c % 2
        B.wait("ACT", "cdv", vx1)
        B.waits("ACT", x1c_free[0])
        vcast = B.op("ACT", lambda e, c=c: e.activation(
            x1c_bf[:], x1f[:, c * 1024:(c + 1) * 1024], AF.Copy), inc="cac")
        B.wait("PE", "cac", vcast)
        B.waits("PE", qwait(0))
        B.wait("PE", "cdv", v_idb)
        vxt = None
        for f in range(KB):
            fn = lambda e, f=f: e.matmul(
                quad[0][:, f * 128:(f + 1) * 128],
                x1c_bf[:, f * 128:(f + 1) * 128], id_bf[:],
                start=True, stop=True)
            if f < KB - 1:
                B.op("PE", fn)
            else:
                vxt = B.op("PE", fn, inc="cpe")
        x1c_free[0] = [("cpe", vxt)]
        B.wait("ACT", "cpe", vxt)
        vxe = B.op("ACT", lambda e, c=c: e.activation(
            x1t_bf[:, c * 1024:(c + 1) * 1024], quad[0][:], AF.Copy),
            inc="cac")
        x1t_evac[c] = vxe
        B.wait("PE", "cac", vxe)
        B.wait("PE", "cdv", v_gwb)
        for k in range(KB):
            B.op("PE", lambda e, k=k, c=c: e.matmul(
                quad[0][:, 0:8],
                x1t_bf[:, c * 1024 + k * 128: c * 1024 + (k + 1) * 128],
                gwT_b[:, k * 8:(k + 1) * 8],
                start=(k == 0), stop=False))
        vg = B.op("PE", lambda e: e.matmul(
            quad[0][:, 0:8], ones_r[0:1, 0:128], gb_s[0:1, :],
            start=False, stop=True), inc="cpe")
        B.wait("DVE", "cpe", vg)
        vneg = B.op("DVE", lambda e, j=j: e.reduce_max(
            negm[j][:], quad[0][:, 0:8], AX.X, negate=True), inc="cdv")
        B.wait("ACT", "cdv", vneg)
        B.wait("ACT", "cpe", vg)
        veg = B.op("ACT", lambda e, j=j: e.activation(
            eg[j][:], quad[0][:, 0:8], AF.Exp, bias=negm[j][:, 0:1],
            accum_out=sg[j][:]), inc="cac")
        B.wait("DVE", "cac", veg)
        vrg = B.op("DVE", lambda e, j=j: e.reciprocal(rg[j][:], sg[j][:]),
                   inc="cdv")
        B.wait("DVE", "cdv", vrg)
        vgt = B.op("DVE", lambda e, j=j: e.tensor_scalar(
            gt[j][:], eg[j][:], rg[j][:, 0:1], None, ALU.mult), inc="cdv")
        B.wait("DVE", "cdv", vgt)
        vt8 = B.op("DVE", lambda e, j=j: e.max(t8[j][:], gt[j][:]), inc="cdv")
        B.wait("DVE", "cdv", vt8)
        vw = B.op("DVE", lambda e, c=c, j=j: e.scalar_tensor_tensor(
            w_all[:, c * 8:(c + 1) * 8], gt[j][:], t8[j][:, 1:2], gt[j][:],
            ALU.is_ge, ALU.mult), inc="cdv")
        w_done[c] = vw
        if s == 0 and c == 0:
            emit_tap("w", w_all[:, 0:8], [("cdv", vw)])
        B.wait("DVE", "cdv", vw)
        vwb = B.op("DVE", lambda e, c=c: e.tensor_copy(
            w_bfs[:, c * 8:(c + 1) * 8], w_all[:, c * 8:(c + 1) * 8]),
            inc="cdv")
        w_bf_done[c] = vwb
        B.wait("PE", "cdv", vwb)
        B.wait("PE", "cac", veg)
        B.wait("PE", "cdv", vneg)
        vtr = B.op("PE", lambda e, c=c: e.matmul(
            quad[0][0:8, 0:128], w_bfs[:, c * 8:(c + 1) * 8], id_bf[:],
            start=True, stop=True), inc="cpe")
        B.wait("DVE", "cpe", vtr)
        vev = B.op("DVE", lambda e, c=c: e.tensor_copy(
            wT_all[:, c * 128:(c + 1) * 128], quad[0][0:8, 0:128]), inc="cdv")
        wT_done[c] = vev
        qset(0, (0,), [("cdv", vev)])

    ffnq = []
    ffn_done_c = [-1]

    def acc_wait_dve(c):
        if stt_val[c] is None:
            B.wait("DVE", "cdv", x1_apply[c])
        else:
            B.wait("DVE", "cdv", stt_val[c])

    def emit_ffn_bias(c, qi):
        B.waits("PE", qwait(qi))
        B.wait("PE", "cdv", wT_done[c])
        B.wait("PE", "cdv", v_ebc1)
        vpe = None
        for hh in range(2):
            fn = lambda e, c=c, qi=qi, hh=hh: e.matmul(
                quad[qi][:, hh * 512:(hh + 1) * 512],
                wT_all[:, c * 128:(c + 1) * 128],
                ebias8[:, hh * 512:(hh + 1) * 512],
                start=True, stop=True)
            if hh == 0:
                B.op("PE", fn)
            else:
                vpe = B.op("PE", fn, inc="cpe")
        B.wait("DVE", "cpe", vpe)
        acc_wait_dve(c)
        B.wait("DVE", "dtap", tap_guard[0])
        vstt = B.op("DVE", lambda e, c=c, qi=qi: e.scalar_tensor_tensor(
            x1f[:, c * 1024:(c + 1) * 1024], quad[qi][:], 1.0,
            x1f[:, c * 1024:(c + 1) * 1024], ALU.mult, ALU.add), inc="cdv")
        stt_val[c] = vstt
        qset(qi, (0, 1), [("cdv", vstt)])

    def emit_ffn_exp(s, e_i, c, qi):
        mat = 4 + e_i
        src = wb[wslot[mat]]
        B.waits("PE", qwait(qi))
        B.waits("PE", wready[mat])
        B.wait("PE", "cac", x1t_evac[c])
        vpe = None
        for k in range(KB):
            for hh in range(2):
                fn = lambda e, src=src, k=k, c=c, qi=qi, hh=hh: e.matmul(
                    quad[qi][:, hh * 512:(hh + 1) * 512],
                    x1t_bf[:, c * 1024 + k * 128: c * 1024 + (k + 1) * 128],
                    src[:, k * 1024 + hh * 512: k * 1024 + (hh + 1) * 512],
                    start=(k == 0), stop=(k == KB - 1))
                if hh == 1 and k == KB - 1:
                    vpe = B.op("PE", fn, inc="cpe")
                else:
                    B.op("PE", fn)
        B.wait("DVE", "cpe", vpe)
        B.wait("DVE", "cdv", w_done[c])
        acc_wait_dve(c)
        B.wait("DVE", "dtap", tap_guard[0])
        vstt = B.op("DVE", lambda e, c=c, qi=qi, e_i=e_i:
                    e.scalar_tensor_tensor(
                        x1f[:, c * 1024:(c + 1) * 1024],
                        quad[qi][:],
                        w_all[:, c * 8 + e_i: c * 8 + e_i + 1],
                        x1f[:, c * 1024:(c + 1) * 1024],
                        ALU.mult, ALU.add), inc="cdv")
        stt_val[c] = vstt
        qset(qi, (0, 1), [("cdv", vstt)])
        if s == 0 and c == 0 and e_i == NE - 1:
            emit_tap("ffn", x1f[:, 0:1024], [("cdv", vstt)])

    ffn_qi = [0]
    s_cur = [0]
    ffnq_owner = [0]
    exp_cnt = [0] * NE

    def ffn_drain(s, limit=None):
        n = 0
        so = ffnq_owner[0]
        while ffnq and (limit is None or n < limit):
            pick = None
            step = ffnq[0]
            c = step[-1]
            if c <= ffn_done_c[0]:
                pick = 0
            if pick is None:
                if limit is None:
                    raise RuntimeError("ffn_drain: no ready step on full drain")
                break
            step = ffnq.pop(pick)
            c = step[-1]
            qi = ffn_qi[0]
            ffn_qi[0] ^= 1
            if step[0] == "b":
                emit_ffn_bias(c, qi)
            else:
                _, e_i, c = step
                emit_ffn_exp(so, e_i, c, qi)
                exp_cnt[e_i] += 1
                if exp_cnt[e_i] == CPS:
                    free_and_feed(4 + e_i)
            n += 1

    def ln2_out(s):
        for c in range(CPS):
            g = 4 * s + c
            j = c % 2
            B.wait("DVE", "cdv", stt_val[c])
            B.op("DVE", lambda e, c=c: e.bn_stats(
                lnst2[:, 0:6], x1f[:, c * 1024: c * 1024 + 512]), inc="cdv")
            vb = B.op("DVE", lambda e, c=c: e.bn_stats(
                lnst2[:, 6:12], x1f[:, c * 1024 + 512: (c + 1) * 1024]),
                inc="cdv")
            B.wait("DVE", "cdv", vb)
            vagg = B.op("DVE", lambda e: e.bn_aggr(mnv2[:], lnst2[:]), inc="cdv")
            B.wait("ACT", "cdv", vagg)
            vsq = B.op("ACT", lambda e: e.activation(
                sdev2[:], mnv2[:, 1:2], AF.Sqrt, bias=eps_c[:, 0:1]), inc="cac")
            B.wait("DVE", "cac", vsq)
            vrs = B.op("DVE", lambda e: e.reciprocal(rstd2[:], sdev2[:]),
                       inc="cdv")
            B.wait("DVE", "cdv", vrs)
            B.waits("DVE", osb_free[j])
            vo = B.op("DVE", lambda e, c=c, j=j: e.tensor_scalar(
                o_sb[j][:], x1f[:, c * 1024:(c + 1) * 1024],
                mnv2[:, 0:1], rstd2[:, 0:1], ALU.subtract, op1=ALU.mult),
                inc="cdv")
            stt_guard[c] = vo
            B.wait("POOL", "cdv", vo)
            vdma = dma("POOL", f"dos{j}", dout[g * 128:(g + 1) * 128, :],
                       o_sb[j][:])
            osb_free[j] = [(f"dos{j}", vdma)]

    nsc = DEBUG_SC if DEBUG_SC is not None else SC
    stage = DEBUG_STAGE if DEBUG_STAGE is not None else 99
    stage_holder[0] = stage
    x_stream(0)
    if stage >= 1:
        feed_weight(0)
        feed_weight(1)
        feed_weight(3)
    fed_b2 = [False]
    pend_ln2 = [None]
    for s in range(nsc if stage >= 1 else 0):
        s_cur[0] = s
        proj_core(s)
        if pend_ln2[0] is not None and stage >= 5:
            ffn_drain(pend_ln2[0])
            ln2_out(pend_ln2[0])
            pend_ln2[0] = None
        xtbf_free[0] = [("cpe", B.val("cpe"))]
        if s == 0:
            emit_tap("qt", qt_bf[:], [("cdv", qt_done[KB - 1])])
            emit_tap("kt", kt_bf[:], [("cdv", kt_done[KB - 1])])
            emit_tap("ktm", ktm_bf[:], [ktm_done[CPS - 1]])
            emit_tap("vext", vext_bf[:], [("cac", vext_done[CPS - 1])])
        prev_attn_evs = []
        if stage >= 4:
            ffn_done_c[0] = -1
            ffnq_owner[0] = s
            for cc in range(CPS):
                stt_val[cc] = None
                ffnq.append(("b", cc))
            for e_i in range(NE):
                exp_cnt[e_i] = 0
                for cc in range(CPS):
                    ffnq.append(("e", e_i, cc))
        if stage >= 2:
            scores_part(s, 0)
            for c in range(CPS):
                sz_part(s, c)
                if c > 0 and stage >= 3:
                    tail_chunk(s, c - 1, prev_attn_evs)
                prev_attn_evs = numden_div(s, c)
                if c < CPS - 1:
                    scores_part(s, c + 1)
                if c > 0 and stage >= 3:
                    tail_b(s, c - 1)
                    ffn_done_c[0] = c - 1
                    if stage >= 4:
                        ffn_drain(s, limit=3)
            if stage >= 3:
                tail_chunk(s, CPS - 1, prev_attn_evs)
                tail_b(s, CPS - 1)
                ffn_done_c[0] = CPS - 1
        if s + 1 < nsc:
            x_stream(s + 1)
        if s == 0 and stage >= 3:
            emit_tap("x1full", x1f[:], [("cdv", x1_apply[cc]) for cc in range(CPS)])
            emit_tap("wfull", w_all[:], [("cdv", w_done[cc]) for cc in range(CPS)])
        if stage >= 4:
            hold = 6 if s + 1 < nsc else 0
            ffn_drain(s, limit=max(0, len(ffnq) - hold))
        if s == 0 and not fed_b2[0]:
            wbuf_free[2] = list(land_free[0])
            feed_weight(2)
            fed_b2[0] = True
        if stage >= 5:
            if s + 1 < nsc:
                pend_ln2[0] = s
            else:
                ffn_drain(s)
                ln2_out(s)

    for cn in ("dxt0", "dxt1", "dx00", "dx01", "dos0", "dos1"):
        B.wait("POOL", cn, B.val(cn))
    for cn in ("dmc", "dwl0", "dwl1", "dwl2", "dwl3", "dscr", "dtap"):
        B.wait("SP", cn, B.val(cn))
    for cn in ("dwm0", "dwm1", "dwm2", "dwm3"):
        B.wait("POOL", cn, B.val(cn))

    with nc.Block() as block:
        @block.tensor
        def _(eng):
            for f in B.streams["PE"]:
                f(eng)

        @block.vector
        def _(eng):
            for f in B.streams["DVE"]:
                f(eng)

        @block.scalar
        def _(eng):
            for f in B.streams["ACT"]:
                f(eng)

        @block.gpsimd
        def _(eng):
            for f in B.streams["POOL"]:
                f(eng)

        @block.sync
        def _(eng):
            for f in B.streams["SP"]:
                f(eng)

    es.close()
    return nc


def _prep_shared(inputs):
    f = np.float32
    a = lambda v: np.ascontiguousarray(np.asarray(v, f))
    return {
        "wqT": a(np.asarray(inputs["L0_attn_Wq"], f).T),
        "wkT": a(np.asarray(inputs["L0_attn_Wk"], f).T),
        "wvT": a(np.asarray(inputs["L0_attn_Wv"], f).T),
        "woT": a(np.asarray(inputs["L0_attn_Wo"], f).T),
        "bq": a(inputs["L0_attn_bq"]).reshape(1, DM),
        "bk": a(inputs["L0_attn_bk"]).reshape(1, DM),
        "bv": a(inputs["L0_attn_bv"]).reshape(1, DM),
        "bo": a(inputs["L0_attn_bo"]).reshape(1, DM),
        "gwT": a(np.asarray(inputs["L0_ffn_gate_w"], f).T),
        "gb": a(inputs["L0_ffn_gate_b"]).reshape(1, NE),
        "ewT": a(np.asarray(inputs["L0_ffn_expert_w"], f).transpose(0, 2, 1)
                 ).reshape(NE * DM, DM),
        "eb": a(inputs["L0_ffn_expert_b"]).reshape(1, NE * DM),
        "ident": np.eye(128, dtype=f),
        "maskT": np.triu(np.ones((128, 128), dtype=f)),
        "pmask": np.stack([(np.arange(128) < 64).astype(f),
                           (np.arange(128) >= 64).astype(f)], axis=1),
        "bqc": a(inputs["L0_attn_bq"]).reshape(KB, 128).T,
        "bkc": a(inputs["L0_attn_bk"]).reshape(KB, 128).T,
    }


def kernel(**inputs):
    x = np.asarray(inputs["L0_attn_x"], np.float32)
    sh = _prep_shared(inputs)
    nc = build()
    in_maps = []
    for i in range(8):
        m = dict(sh)
        m["x"] = np.ascontiguousarray(x[i])
        m["xT"] = np.ascontiguousarray(x[i].T)
        in_maps.append(m)
    res = run_bass_kernel_spmd(nc, in_maps, core_ids=list(range(8)))
    return np.stack([res.results[i]["out"] for i in range(8)], axis=0)
